# version 02
# baseline (speedup 1.0000x reference)
import numpy as np
import ml_dtypes
from contextlib import ExitStack

import concourse.bass as bass
import concourse.mybir as mybir
from concourse.bass_utils import run_bass_kernel_spmd

F32 = mybir.dt.float32
BF16 = mybir.dt.bfloat16
AF = mybir.ActivationFunctionType
ALU = mybir.AluOpType
AX = mybir.AxisListType
BF = ml_dtypes.bfloat16

NCORES = 8
N = 2048
DM = 1024
NT = 16
ND = 8

ALL_PARTS = ["l0aa", "l0ffn", "l0ab", "l0ffn", "l1aa", "l1ffn", "l1ab", "l1ffn"]

CP = {
    "eps": 0, "scale_attn": 1, "inv_d": 2, "one_p_eps": 3, "eps_sq": 4,
    "inv_4096": 5, "inv_1024": 6, "inv_128": 7, "lam_init": 8, "fifth": 9, "inv_32": 10,
}
_CPN = 16
_CP_VALS = {
    "eps": 1e-5, "scale_attn": 0.125, "inv_d": 1.0 / 1024, "one_p_eps": 1.0 + 1e-5,
    "eps_sq": 1e-10, "inv_4096": 1.0 / 4096, "inv_1024": 1.0 / 1024, "inv_128": 1.0 / 128,
    "lam_init": 0.8, "fifth": 0.2, "inv_32": 1.0 / 32,
}



ENG_KEYS = ("t", "v", "s", "g", "y")
SEM_NAMES = ("te", "ve", "se", "pe", "qs", "qg", "qc")
_QUEUE_SEM = {"y": "qs", "g": "qg", "s": "qc"}


class Prog:
    def __init__(self, nc):
        self.nc = nc
        self.q = {e: [] for e in ENG_KEYS}
        self.sems = {}
        self.cnt = {s: 0 for s in SEM_NAMES}
        for s in SEM_NAMES:
            self.sems[s] = nc.alloc_semaphore(name="sem_" + s)

    def w(self, e, sem, tgt):
        if tgt is None or tgt <= 0:
            return
        assert tgt <= self.cnt[sem], f"wait on future {sem}: {tgt} > {self.cnt[sem]}"
        h = self.sems[sem]
        self.q[e].append(lambda eng, h=h, tgt=tgt: eng.wait_ge(h, tgt))

    def i(self, e, fn, inc=None, n=1):
        if inc is not None:
            self.cnt[inc] += n
            h = self.sems[inc]
            self.q[e].append(lambda eng, fn=fn, h=h, n=n: fn(eng).then_inc(h, n))
            return self.cnt[inc]
        self.q[e].append(fn)
        return None

    def dma(self, qe, out_ap, in_ap):
        sem = _QUEUE_SEM[qe]
        return self.i(qe, lambda eng, o=out_ap, i_=in_ap: eng.dma_start(o, i_), inc=sem, n=16)

    def ms(self, e):
        return {"t": "te", "v": "ve", "s": "se", "g": "pe"}[e]

    def mm(self, out, lhsT, rhs, start=True, stop=True, tp=None, skip=False, inc=None):
        def f(eng, out=out, lhsT=lhsT, rhs=rhs, start=start, stop=stop, tp=tp, skip=skip):
            kw = {}
            if tp is not None:
                kw["tile_position"] = tp
            if skip:
                kw["skip_group_check"] = True
            return eng.matmul(out, lhsT, rhs, start=start, stop=stop, **kw)
        return self.i("t", f, inc="te" if inc else None)

    def tr(self, out, in_, ident, inc=None):
        return self.i("t", lambda eng, o=out, i_=in_, id_=ident: eng.transpose(o, i_, id_),
                      inc="te" if inc else None)

    def cp(self, e, out, in_, inc=None):
        return self.i(e, lambda eng, o=out, i_=in_: eng.tensor_copy(o, i_),
                      inc=self.ms(e) if inc else None)

    def ts(self, e, out, in0, s1, s2, op0, op1=ALU.bypass, inc=None):
        def f(eng, out=out, in0=in0, s1=s1, s2=s2, op0=op0, op1=op1):
            return eng.tensor_scalar(out, in0, s1, s2, op0, op1)
        return self.i(e, f, inc=self.ms(e) if inc else None)

    def tt(self, e, out, in0, in1, op, inc=None):
        return self.i(e, lambda eng, o=out, a=in0, b=in1, op=op: eng.tensor_tensor(o, a, b, op),
                      inc=self.ms(e) if inc else None)

    def stt(self, e, out, in0, scal, in1, op0, op1, inc=None):
        def f(eng, out=out, in0=in0, scal=scal, in1=in1, op0=op0, op1=op1):
            return eng.scalar_tensor_tensor(out, in0, scal, in1, op0, op1)
        return self.i(e, f, inc=self.ms(e) if inc else None)

    def red(self, e, out, in_, op, absval=False, inc=None):
        def f(eng, out=out, in_=in_, op=op, absval=absval):
            return eng.tensor_reduce(out, in_, AX.X, op, apply_absolute_value=absval)
        return self.i(e, f, inc=self.ms(e) if inc else None)

    def act(self, out, in_, func, bias=0.0, scale=1.0, accum=None, inc=None):
        def f(eng, out=out, in_=in_, func=func, bias=bias, scale=scale, accum=accum):
            kw = {}
            if accum is not None:
                kw["accum_out"] = accum
            return eng.activation(out, in_, func, bias=bias, scale=scale, **kw)
        return self.i("s", f, inc="se" if inc else None)

    def recip(self, out, in_, inc=None):
        return self.i("v", lambda eng, o=out, i_=in_: eng.reciprocal(o, i_),
                      inc="ve" if inc else None)

    def barrier(self):
        snap = [(s, self.cnt[s]) for s in ("te", "ve", "se")]
        own = {"t": "te", "v": "ve", "s": "se"}
        for e in ENG_KEYS:
            for s, tgt in snap:
                if own.get(e) == s:
                    continue
                self.w(e, s, tgt)


class ColAlloc:
    def __init__(self, ncols):
        self.ncols = ncols
        self.off = 0

    def take(self, n):
        assert self.off + n <= self.ncols, f"pool overflow: {self.off}+{n} > {self.ncols}"
        o = self.off
        self.off += n
        return o



def _rope_tables():
    hd = 64
    inv_freq = 1.0 / 10000.0 ** (np.arange(0, hd, 2, dtype=np.float64) / hd)
    pos = np.arange(N, dtype=np.float64)
    ang = pos[:, None] * inv_freq[None, :]
    sin = np.sin(ang).astype(np.float32).T
    cos = np.cos(ang).astype(np.float32).T
    cos128 = np.concatenate([cos, cos, cos, cos], axis=0)
    sin128 = np.concatenate([-sin, -sin, sin, sin], axis=0)
    return cos128.astype(BF), sin128.astype(BF)


def _stage_common(inp):
    st = {}
    cpool = np.zeros((128, _CPN), dtype=np.float32)
    for k, j in CP.items():
        cpool[:, j] = _CP_VALS[k]
    st["cpool"] = cpool
    st["ident_bf"] = np.eye(128, dtype=BF)
    swap = np.zeros((128, 128), dtype=np.float32)
    swap[np.arange(128), (np.arange(128) + 64) % 128] = 1.0
    st["swap_bf"] = swap.astype(BF)
    st["ones_bf"] = np.ones((128, 1), dtype=BF)
    st["onesr_bf"] = np.ones((1, 128), dtype=BF)
    st["zeros_bf"] = np.zeros((128, 512), dtype=BF)
    qi = np.arange(128)[:, None]
    kj = np.arange(128)[None, :]
    st["cmask"] = np.where(kj <= qi, 0.0, -1e9).astype(np.float32)
    cos128, sin128 = _rope_tables()
    st["rope_cos"] = cos128
    st["rope_sin"] = sin128

    g = np.asarray(inp["L0_attn_a_ln_gamma"], np.float32)
    bet = np.asarray(inp["L0_attn_a_ln_beta"], np.float32)
    st["l0aa_gamma_bc"] = np.ascontiguousarray(np.broadcast_to(g, (128, DM)))
    st["l0aa_beta_bc"] = np.ascontiguousarray(np.broadcast_to(bet, (128, DM)))
    st["l0aa_bout_bc"] = np.ascontiguousarray(
        np.broadcast_to(np.asarray(inp["L0_attn_a_b_out"], np.float32), (128, DM)))
    wqkv = np.asarray(inp["L0_attn_a_w_qkv"], dtype=np.float32)
    wq, wk, wv = wqkv[:, :1024], wqkv[:, 1024:2048], wqkv[:, 2048:]
    cols = []
    for pp in range(8):
        hA, hB = 2 * pp, 2 * pp + 1
        for w in (wq, wk):
            blk = np.concatenate([
                w[:, hA * 64:hA * 64 + 32], w[:, hB * 64:hB * 64 + 32],
                w[:, hA * 64 + 32:hA * 64 + 64], w[:, hB * 64 + 32:hB * 64 + 64]], axis=1)
            cols.append(blk)
        cols.append(np.concatenate([wv[:, hA * 64:(hA + 1) * 64],
                                    wv[:, hB * 64:(hB + 1) * 64]], axis=1))
    st["l0aa_wqkv"] = np.ascontiguousarray(np.concatenate(cols, axis=1))
    st["l0aa_wout"] = np.asarray(inp["L0_attn_a_w_out"], dtype=np.float32)

    w1 = np.asarray(inp["L0_ffn_w1"], dtype=np.float32)
    w2 = np.asarray(inp["L0_ffn_w2"], dtype=np.float32)
    st["l0ffn_w1T"] = np.ascontiguousarray(w1.T)
    st["l0ffn_w2T"] = np.ascontiguousarray(w2.T)
    st["l0ffn_b1c"] = np.ascontiguousarray(
        np.asarray(inp["L0_ffn_b1"], np.float32).reshape(32, 128).T)
    st["l0ffn_b2bc"] = np.ascontiguousarray(
        np.broadcast_to(np.asarray(inp["L0_ffn_b2"], np.float32), (128, DM)))

    st["l0ab_wthT"] = np.ascontiguousarray(np.asarray(inp["L0_attn_b_W_theta"], np.float32).T)
    st["l0ab_wphT"] = np.ascontiguousarray(np.asarray(inp["L0_attn_b_W_phi"], np.float32).T)
    st["l0ab_wgT"] = np.ascontiguousarray(np.asarray(inp["L0_attn_b_W_g"], np.float32).T)

    st["l1aa_wqT"] = np.ascontiguousarray(np.asarray(inp["L1_attn_a_Wq"], np.float32).T)
    st["l1aa_wkT"] = np.ascontiguousarray(np.asarray(inp["L1_attn_a_Wk"], np.float32).T)
    st["l1aa_wvT"] = np.ascontiguousarray(np.asarray(inp["L1_attn_a_Wv"], np.float32).T)
    st["l1aa_woT"] = np.ascontiguousarray(np.asarray(inp["L1_attn_a_Wo"], np.float32).T)

    st["l1ffn_wg"] = np.asarray(inp["L1_ffn_Wg"], np.float32)
    st["l1ffn_bg_bc"] = np.ascontiguousarray(
        np.broadcast_to(np.asarray(inp["L1_ffn_bg"], np.float32), (128, 8)))
    st["l1ffn_w1"] = np.asarray(inp["L1_ffn_W1"], np.float32)
    st["l1ffn_w2"] = np.asarray(inp["L1_ffn_W2"], np.float32)
    b1 = np.asarray(inp["L1_ffn_b1"], np.float32)
    st["l1ffn_b1c"] = np.ascontiguousarray(
        b1.reshape(8, 16, 128).transpose(2, 0, 1).reshape(128, 128))
    st["l1ffn_b2"] = np.asarray(inp["L1_ffn_b2"], np.float32)

    st["l1ab_wqT"] = np.ascontiguousarray(np.asarray(inp["L1_attn_b_Wq"], np.float32).T)
    st["l1ab_wkT"] = np.ascontiguousarray(np.asarray(inp["L1_attn_b_Wk"], np.float32).T)
    st["l1ab_wvT"] = np.ascontiguousarray(np.asarray(inp["L1_attn_b_Wv"], np.float32).T)
    st["l1ab_woT"] = np.ascontiguousarray(np.asarray(inp["L1_attn_b_Wo"], np.float32).T)
    st["l1ab_lq1T"] = np.ascontiguousarray(np.asarray(inp["L1_attn_b_lq1"], np.float32).T)
    st["l1ab_lk1T"] = np.ascontiguousarray(np.asarray(inp["L1_attn_b_lk1"], np.float32).T)
    st["l1ab_lq2T"] = np.ascontiguousarray(np.asarray(inp["L1_attn_b_lq2"], np.float32).T)
    st["l1ab_lk2T"] = np.ascontiguousarray(np.asarray(inp["L1_attn_b_lk2"], np.float32).T)
    st["l1ab_rms"] = np.ascontiguousarray(
        np.asarray(inp["L1_attn_b_rms_scale"], np.float32).reshape(128, 1))
    return st



POOL_COLS = 30720
POOLF_COLS = 2048


class Bufs:
    pass


def _alloc_bufs(nc, ctx):
    b = Bufs()
    sb = lambda name, shape, dt: ctx.enter_context(nc.sbuf_tensor(name, shape, dt))
    b.x = sb("x_sb", [128, NT * DM], F32)
    b.xnt = sb("xnt_sb", [128, ND * N], BF16)
    b.xn = sb("xn_sb", [128, 2 * DM], BF16)
    b.sqs = sb("sqs_sb", [128, DM], BF16)
    b.st = sb("st_sb", [128, 192], F32)
    b.pool = sb("pool_sb", [128, POOL_COLS], BF16)
    b.poolf = sb("poolf_sb", [128, POOLF_COLS], F32)
    b.cp = sb("cp_sb", [128, _CPN], F32)
    b.ident = sb("ident_sb", [128, 128], BF16)
    b.swap = sb("swap_sb", [128, 128], BF16)
    b.ones = sb("ones_sb", [128, 1], BF16)
    b.onesr = sb("onesr_sb", [1, 128], BF16)
    b.zeros = sb("zeros_sb", [128, 512], BF16)
    b.cmask = sb("cmask_sb", [128, 128], F32)
    b.cos = sb("cos_sb", [128, N], BF16)
    b.sin = sb("sin_sb", [128, N], BF16)
    b.gamma = sb("gamma_sb", [128, DM], F32)
    b.beta = sb("beta_sb", [128, DM], F32)
    b.bout = sb("bout_sb", [128, DM], F32)
    b.smf = sb("smf_sb", [128, 512], F32)
    b.ct = sb("ct_sb", [8, N], BF16)
    b.b2e = sb("b2e_sb", [8, DM], BF16)
    b.smb = sb("smb_sb", [128, 2048], BF16)
    b.psa = ctx.enter_context(nc.psum_tensor("psa", [128, 1024], F32))
    b.psb = ctx.enter_context(nc.psum_tensor("psb", [128, 1024], F32))
    b.psc = ctx.enter_context(nc.psum_tensor("psc", [128, 1024], F32))
    b.pt32 = ctx.enter_context(nc.psum_tensor("pt32", [128, 512], F32))
    b.ptb = ctx.enter_context(nc.psum_tensor("ptb", [128, 1024], BF16))
    return b


def cpc(b, name):
    j = CP[name]
    return b.cp[:, j:j + 1]


def xnt_chunk(b, dc, c0, c1):
    return b.xnt[:, dc * N + c0:dc * N + c1]



def emit_ln_xnt(P, b, x_ready_ve, gamma_beta=False):
    st = b.st
    P.w("v", "ve", x_ready_ve)
    P.w("s", "ve", x_ready_ve)
    ve_red = None
    se_sq = None
    for tt in range(NT):
        xt = b.x[:, tt * DM:(tt + 1) * DM]
        ve_red = P.red("v", st[:, tt:tt + 1], xt, ALU.add, inc=(tt == NT - 1))
        se_sq = P.act(b.sqs[:], xt, AF.Square, accum=st[:, 16 + tt:16 + tt + 1],
                      inc=(tt == NT - 1))
    P.w("v", "se", se_sq)
    mu = st[:, 64:80]; ex2 = st[:, 80:96]; var = st[:, 96:112]; sd = st[:, 112:128]
    vt = P.ts("v", mu, st[:, 0:16], cpc(b, "inv_d"), None, ALU.mult, inc=True)
    vt = P.ts("v", ex2, st[:, 16:32], cpc(b, "inv_d"), None, ALU.mult, inc=True)
    P.w("v", "ve", vt)
    vt = P.tt("v", var, mu, mu, ALU.mult, inc=True)
    P.w("v", "ve", vt)
    ve_t = P.tt("v", var, ex2, var, ALU.subtract, inc=True)
    P.w("s", "ve", ve_t)
    if gamma_beta:
        se_t = P.act(sd, var, AF.Sqrt, bias=cpc(b, "eps_sq"), scale=cpc(b, "one_p_eps"), inc=True)
    else:
        se_t = P.act(sd, var, AF.Sqrt, bias=cpc(b, "eps"), inc=True)
    P.w("v", "se", se_t)
    rstd = st[:, 32:48]; negmur = st[:, 48:64]
    vt = P.recip(rstd, sd, inc=True)
    P.w("v", "ve", vt)
    vt = P.tt("v", negmur, mu, rstd, ALU.mult, inc=True)
    P.w("v", "ve", vt)
    ve_t = P.ts("v", negmur, negmur, -1.0, None, ALU.mult, inc=True)
    P.w("v", "ve", ve_t)

    te_tr = {}
    ve_t = None
    for tt in range(NT):
        xt = b.x[:, tt * DM:(tt + 1) * DM]
        xn = b.xn[:, (tt % 2) * DM:(tt % 2 + 1) * DM]
        if tt - 2 in te_tr:
            P.w("v", "te", te_tr[tt - 2])
        if gamma_beta:
            tmp = b.poolf[:, 0:DM]
            P.ts("v", tmp, xt, st[:, 32 + tt:32 + tt + 1], st[:, 48 + tt:48 + tt + 1],
                 ALU.mult, ALU.add)
            P.stt("v", tmp, tmp, 1.0, b.gamma[:], ALU.mult, ALU.mult)
            ve_t = P.tt("v", xn, tmp, b.beta[:], ALU.add, inc=True)
        else:
            ve_t = P.ts("v", xn, xt, st[:, 32 + tt:32 + tt + 1], st[:, 48 + tt:48 + tt + 1],
                        ALU.mult, ALU.add, inc=True)
        P.w("t", "ve", ve_t)
        te_l = None
        for dc in range(ND):
            te_l = P.tr(b.ptb[:, dc * 128:(dc + 1) * 128], xn[:, dc * 128:(dc + 1) * 128],
                        b.ident[:], inc=(dc == ND - 1))
        te_tr[tt] = te_l
        P.w("v", "te", te_l)
        xnt_view = b.xnt[:].rearrange("p (dc n) -> p dc n", dc=ND)[:, :, tt * 128:(tt + 1) * 128]
        ptb_view = b.ptb[:].rearrange("p (dc j) -> p dc j", dc=ND)
        ve_t = P.cp("v", xnt_view, ptb_view, inc=True)
        P.w("t", "ve", ve_t)
    return ve_t



def emit_l0ffn(P, b, dr, xnt_ve):
    NQ = 4
    HCQ = 8
    TG = 256
    NTG = N // TG

    pool = ColAlloc(POOL_COLS)
    R1 = [pool.take(8192), pool.take(8192)]
    R2 = [pool.take(4096), pool.take(4096)]
    pool = ColAlloc(POOL_COLS)
    R1 = [pool.take(8192), pool.take(8192)]
    R2 = [pool.take(8192)]
    RH = [pool.take(2048), pool.take(2048)]
    RSC = pool.take(1024)
    A2BC = pool.take(1024)

    w1T, w2T = dr["w1T"], dr["w2T"]
    A1 = 0
    b1c = b.smf[:, 64:96]
    t_sm = P.dma("y", b1c, dr["b1c"])
    t_sm = P.dma("y", b.bout[:], dr["b2bc"])
    qs_smalls = t_sm

    te_a2 = None
    qg_prev_read_se = None
    for q in range(NQ):
        reg = R2[0] if q % 2 == 0 else R1[1]
        if qg_prev_read_se is not None:
            P.w("g", "se", qg_prev_read_se)
        qg_t = None
        for hcq in range(HCQ):
            hc = q * HCQ + hcq
            qg_t = P.dma("g", b.pool[:, reg + hcq * 1024: reg + (hcq + 1) * 1024],
                         w2T[hc * 128:(hc + 1) * 128, :])
        P.w("s", "qg", qg_t)
        ve_l = None
        for hcq in range(HCQ):
            hc = q * HCQ + hcq
            wchunk = b.pool[:, reg + hcq * 1024: reg + (hcq + 1) * 1024]
            for cg in range(2):
                scr = b.pool[:, RSC + ((hcq * 2 + cg) % 2) * 512: RSC + ((hcq * 2 + cg) % 2) * 512 + 512]
                if te_a2 is not None:
                    P.w("s", "te", te_a2)
                ve_l = P.act(scr, wchunk[:, cg * 512:(cg + 1) * 512], AF.Abs, inc=True)
                P.w("t", "se", ve_l)
                te_a2 = P.mm(b.psa[0:1, cg * 512:(cg + 1) * 512], b.ones[:], scr,
                             start=(hc == 0), stop=(hc == 4 * HCQ - 1), inc=True)
        qg_prev_read_se = ve_l
    P.w("v", "te", te_a2)
    a2row = b.smb[0:1, 0:1024]
    ve_t = P.ts("v", a2row, b.psa[0:1, 0:1024], 1.0 / 4096.0, None, ALU.mult, inc=True)
    P.w("t", "ve", ve_t)
    te_t = None
    for g2 in range(2):
        te_t = P.mm(b.pt32[:, 0:512], b.onesr[:], a2row[:, g2 * 512:(g2 + 1) * 512],
                    inc=True)
        P.w("v", "te", te_t)
        ve_t = P.cp("v", b.pool[:, A2BC + g2 * 512: A2BC + (g2 + 1) * 512], b.pt32[:, 0:512],
                    inc=True)
        P.w("t", "ve", ve_t)
    ve_a2bc = ve_t

    P.w("s", "qs", qs_smalls)
    prepass_se = qg_prev_read_se
    te_w1_used = [None, None]
    te_w2_used = None
    se_hT_slot = {}
    ve_y_slot = {}
    ve_x = None
    for q in range(NQ):
        p1 = q % 2
        reg1, reg2 = R1[p1], R2[0]
        if te_w1_used[p1] is not None:
            P.w("g", "te", te_w1_used[p1])
        elif p1 == 1:
            P.w("g", "se", prepass_se)
        t1 = None
        for dc in range(ND):
            t1 = P.dma("g", b.pool[:, reg1 + dc * 1024: reg1 + (dc + 1) * 1024],
                       w1T[dc * 128:(dc + 1) * 128, q * 1024:(q + 1) * 1024])
        if te_w2_used is not None:
            P.w("g", "te", te_w2_used)
        elif q == 0:
            P.w("g", "se", prepass_se)
        t2 = None
        for hcq in range(HCQ):
            hc = q * HCQ + hcq
            t2 = P.dma("g", b.pool[:, reg2 + hcq * 1024: reg2 + (hcq + 1) * 1024],
                       w2T[hc * 128:(hc + 1) * 128, :])

        P.w("s", "qg", t1)
        te_a1 = None
        se_abs = None
        for cg in range(2):
            for dc in range(ND):
                wchunk = b.pool[:, reg1 + dc * 1024: reg1 + (dc + 1) * 1024]
                scr = b.pool[:, RSC + (dc % 2) * 512: RSC + (dc % 2) * 512 + 512]
                if te_a1 is not None:
                    P.w("s", "te", te_a1)
                se_abs = P.act(scr, wchunk[:, cg * 512:(cg + 1) * 512], AF.Abs, inc=True)
                P.w("t", "se", se_abs)
                te_a1 = P.mm(b.psb[0:1, cg * 512:(cg + 1) * 512], b.ones[:], scr,
                             start=(dc == 0), stop=(dc == ND - 1), inc=True)
        P.w("v", "te", te_a1)
        a1row = b.smb[0:1, 1024:2048]
        ve_t = P.ts("v", a1row, b.psb[0:1, 0:1024], 1.0 / 1024.0, None, ALU.mult, inc=True)
        P.w("t", "ve", ve_t)
        for hcq in range(HCQ):
            te_t = P.mm(b.pt32[:, 64 + hcq:64 + hcq + 1],
                        a1row[:, hcq * 128:(hcq + 1) * 128], b.ones[0:1, 0:1],
                        inc=(hcq == HCQ - 1))
        P.w("v", "te", te_t)
        ve_a1 = P.cp("v", b.smf[:, A1 + q * 8:A1 + q * 8 + 8], b.pt32[:, 64:72], inc=True)

        se_sign = None
        for dc in range(ND):
            wchunk = b.pool[:, reg1 + dc * 1024: reg1 + (dc + 1) * 1024]
            se_sign = P.act(wchunk, wchunk, AF.Sign, inc=True)
        P.w("s", "qg", t2)
        for hcq in range(HCQ):
            wchunk = b.pool[:, reg2 + hcq * 1024: reg2 + (hcq + 1) * 1024]
            se_sign = P.act(wchunk, wchunk, AF.Sign, inc=True)
        P.w("v", "se", se_sign)
        P.w("v", "ve", ve_a2bc) if q == 0 else None
        ve_bw2 = None
        for hcq in range(HCQ):
            wchunk = b.pool[:, reg2 + hcq * 1024: reg2 + (hcq + 1) * 1024]
            ve_bw2 = P.tt("v", wchunk, wchunk, b.pool[:, A2BC:A2BC + 1024], ALU.mult, inc=True)

        P.w("t", "se", se_sign)
        P.w("t", "ve", ve_bw2)
        if q == 0:
            P.w("t", "ve", xnt_ve)
            P.w("s", "ve", ve_a1)
        else:
            P.w("s", "ve", ve_a1)
        te_last = None
        for tg in range(NTG):
            hreg = RH[tg % 2]
            for hcq in range(HCQ):
                slot = hcq % 2
                ps = b.psc[:, slot * 512: slot * 512 + 256]
                if (slot, "h") in se_hT_slot:
                    P.w("t", "se", se_hT_slot[(slot, "h")])
                for dc in range(ND):
                    lhs = b.pool[:, reg1 + dc * 1024 + hcq * 128: reg1 + dc * 1024 + (hcq + 1) * 128]
                    te_t = P.mm(ps, lhs, xnt_chunk(b, dc, tg * TG, (tg + 1) * TG),
                                start=(dc == 0), stop=(dc == ND - 1), inc=(dc == ND - 1))
                P.w("s", "te", te_t)
                hc_glob = q * HCQ + hcq
                se_t = P.act(b.pool[:, hreg + hcq * 256: hreg + (hcq + 1) * 256], ps, AF.Gelu,
                             bias=b1c[:, hc_glob:hc_glob + 1],
                             scale=b.smf[:, A1 + q * 8 + hcq:A1 + q * 8 + hcq + 1], inc=True)
                se_hT_slot[(slot, "h")] = se_t
            P.w("t", "se", se_t)
            for tsub in range(2):
                tt_tok = tg * 2 + tsub
                for Dg in range(2):
                    slot = (tsub * 2 + Dg) % 2
                    ps = b.psb[:, slot * 512: (slot + 1) * 512]
                    if (slot, "y") in ve_y_slot:
                        P.w("t", "ve", ve_y_slot[(slot, "y")])
                    for hcq in range(HCQ):
                        lhs = b.pool[:, hreg + hcq * 256 + tsub * 128: hreg + hcq * 256 + (tsub + 1) * 128]
                        rhs = b.pool[:, reg2 + hcq * 1024 + Dg * 512: reg2 + hcq * 1024 + (Dg + 1) * 512]
                        te_t = P.mm(ps, lhs, rhs, start=(hcq == 0), stop=(hcq == HCQ - 1),
                                    inc=(hcq == HCQ - 1))
                    te_last = te_t
                    P.w("v", "te", te_t)
                    xsl = b.x[:, tt_tok * DM + Dg * 512: tt_tok * DM + (Dg + 1) * 512]
                    ve_t = P.tt("v", xsl, ps, xsl, ALU.add, inc=True)
                    ve_y_slot[(slot, "y")] = ve_t
        te_w1_used[p1] = te_last
        te_w2_used = te_last
        ve_x = ve_t

    P.w("v", "qs", qs_smalls)
    for tt in range(NT):
        xsl = b.x[:, tt * DM:(tt + 1) * DM]
        ve_x = P.tt("v", xsl, xsl, b.bout[:], ALU.add, inc=True)
    return ve_x




def emit_l0ab(P, b, dr, xnt_ve):
    pool = ColAlloc(POOL_COLS)
    RWT = pool.take(2048)
    RWP = pool.take(2048)
    RWG = pool.take(4096)
    RTH = pool.take(4096)
    RPH = pool.take(4096)
    RG = pool.take(8192)
    RM = pool.take(2048)

    wthT, wphT, wgT = dr["wthT"], dr["wphT"], dr["wgT"]
    t1 = None
    for dc in range(ND):
        t1 = P.dma("g", b.pool[:, RWT + dc * 256: RWT + (dc + 1) * 256],
                   wthT[dc * 128:(dc + 1) * 128, :])
    for dc in range(ND):
        t1 = P.dma("g", b.pool[:, RWP + dc * 256: RWP + (dc + 1) * 256],
                   wphT[dc * 128:(dc + 1) * 128, :])
    P.w("t", "qg", t1)
    P.w("t", "ve", xnt_ve)

    ve_th = None
    for Lc in range(2):
        for tg in range(4):
            ps = b.psc[:, (tg % 2) * 512:(tg % 2) * 512 + 512]
            if ve_th is not None and (Lc * 4 + tg) >= 2:
                P.w("t", "ve", ve_th_slot[(tg % 2)])
            for dc in range(ND):
                lhs = b.pool[:, RWT + dc * 256 + Lc * 128: RWT + dc * 256 + (Lc + 1) * 128]
                te_t = P.mm(ps, lhs, xnt_chunk(b, dc, tg * 512, (tg + 1) * 512),
                            start=(dc == 0), stop=(dc == ND - 1), inc=(dc == ND - 1))
            P.w("v", "te", te_t)
            ve_th = P.cp("v", b.pool[:, RTH + Lc * 2048 + tg * 512: RTH + Lc * 2048 + (tg + 1) * 512],
                         ps, inc=True)
            if Lc * 4 + tg == 0:
                ve_th_slot = {}
            ve_th_slot[(tg % 2)] = ve_th
    ve_ph_slot = {}
    ve_ph = None
    for tt in range(NT):
        ps = b.psc[:, (tt % 2) * 512:(tt % 2) * 512 + 256]
        P.w("t", "ve", ve_th_slot[tt % 2] if tt < 2 else ve_ph_slot[tt % 2])
        for dc in range(ND):
            lhs = xnt_chunk(b, dc, tt * 128, (tt + 1) * 128)
            rhs = b.pool[:, RWP + dc * 256: RWP + (dc + 1) * 256]
            te_t = P.mm(ps, lhs, rhs, start=(dc == 0), stop=(dc == ND - 1), inc=(dc == ND - 1))
        P.w("v", "te", te_t)
        ve_ph = P.cp("v", b.pool[:, RPH + tt * 256: RPH + (tt + 1) * 256], ps, inc=True)
        ve_ph_slot[tt % 2] = ve_ph

    ve_m = {}
    te_g_last = None
    for og in range(2):
        tg_w = None
        if og == 1:
            P.w("g", "te", te_g_last)
        for dc in range(ND):
            tg_w = P.dma("g", b.pool[:, RWG + dc * 512: RWG + (dc + 1) * 512],
                         wgT[dc * 128:(dc + 1) * 128, og * 512:(og + 1) * 512])
        P.w("t", "qg", tg_w)
        ve_g_slot = {}
        for tt in range(NT):
            ps = b.psc[:, (tt % 2) * 512:(tt % 2) * 512 + 512]
            if tt >= 2:
                P.w("t", "ve", ve_g_slot[tt % 2])
            elif og == 0:
                P.w("t", "ve", ve_ph_slot[tt % 2])
            for dc in range(ND):
                lhs = xnt_chunk(b, dc, tt * 128, (tt + 1) * 128)
                rhs = b.pool[:, RWG + dc * 512: RWG + (dc + 1) * 512]
                te_t = P.mm(ps, lhs, rhs, start=(dc == 0), stop=(dc == ND - 1), inc=(dc == ND - 1))
            P.w("v", "te", te_t)
            ve_g = P.cp("v", b.pool[:, RG + tt * 512: RG + (tt + 1) * 512], ps, inc=True)
            ve_g_slot[tt % 2] = ve_g
        P.w("t", "ve", ve_g)
        for Lc in range(2):
            psm = (b.psa if Lc == 0 else b.psb)[:, og * 512:(og + 1) * 512]
            for tt in range(NT):
                lhs = b.pool[:, RPH + tt * 256 + Lc * 128: RPH + tt * 256 + (Lc + 1) * 128]
                rhs = b.pool[:, RG + tt * 512: RG + (tt + 1) * 512]
                te_g_last = P.mm(psm, lhs, rhs, start=(tt == 0), stop=(tt == NT - 1),
                                 inc=(tt == NT - 1))
            ve_m[(Lc, og)] = te_g_last
    ve_t = None
    for Lc in range(2):
        for og in range(2):
            P.w("v", "te", ve_m[(Lc, og)])
            ve_t = P.cp("v", b.pool[:, RM + Lc * 1024 + og * 512: RM + Lc * 1024 + (og + 1) * 512],
                        (b.psa if Lc == 0 else b.psb)[:, og * 512:(og + 1) * 512], inc=True)
    ve_mev = ve_t

    P.w("t", "ve", ve_mev)
    ve_x = None
    ve_o_slot = {}
    for tt in range(NT):
        for og in range(2):
            sl = (tt * 2 + og) % 2
            ps = b.psc[:, sl * 512: sl * 512 + 512]
            if (sl in ve_o_slot):
                P.w("t", "ve", ve_o_slot[sl])
            for Lc in range(2):
                lhs = b.pool[:, RTH + Lc * 2048 + tt * 128: RTH + Lc * 2048 + (tt + 1) * 128]
                rhs = b.pool[:, RM + Lc * 1024 + og * 512: RM + Lc * 1024 + (og + 1) * 512]
                te_t = P.mm(ps, lhs, rhs, start=(Lc == 0), stop=(Lc == 1), inc=(Lc == 1))
            P.w("v", "te", te_t)
            xsl = b.x[:, tt * DM + og * 512: tt * DM + (og + 1) * 512]
            ve_x = P.stt("v", xsl, ps, 1.0 / 32.0, xsl, ALU.mult, ALU.add, inc=True)
            ve_o_slot[sl] = ve_x
    return ve_x




def emit_l1ffn(P, b, dr, xnt_ve):
    pool = ColAlloc(POOL_COLS)
    RW1 = [pool.take(8192), pool.take(8192)]
    RW2 = pool.take(8192)
    RH = [pool.take(2048), pool.take(2048)]
    RWG = pool.take(64)
    RWG2 = pool.take(64)
    RCB = pool.take(8)
    RXL = pool.take(1024)

    CMF = 352
    B1C = 96
    BGB = 224

    t = P.dma("y", b.smf[:, B1C:B1C + 128], dr["mb1c"])
    t = P.dma("y", b.smf[:, BGB:BGB + 8], dr["mbg"])
    wgf = b.poolf[:, 1080:1144]
    for dc in range(ND):
        t = P.dma("y", wgf[:, dc * 8:(dc + 1) * 8], dr["mwg"][dc * 128:(dc + 1) * 128, :])
    qs_sm = t
    tg_w = None
    for dc in range(ND):
        tg_w = P.dma("g", b.pool[:, RWG + dc * 8: RWG + (dc + 1) * 8],
                     dr["mwg"][dc * 128:(dc + 1) * 128, :])
    tg_w = P.dma("g", b.b2e[:], dr["mb2"])
    qg_small = tg_w

    P.w("t", "ve", xnt_ve)
    P.w("t", "qg", qg_small)
    P.w("v", "qs", qs_sm)
    lg = b.poolf[:, 1024:1032]; ex = b.poolf[:, 1032:1040]; pp = b.poolf[:, 1040:1048]
    pm = b.poolf[:, 1048:1056]; cu = b.poolf[:, 1056:1064]; tmp = b.poolf[:, 1064:1072]
    rmax = b.poolf[:, 1072:1073]; nrm = b.poolf[:, 1073:1074]; esum = b.poolf[:, 1074:1075]
    res_ = b.poolf[:, 1075:1076]; m1 = b.poolf[:, 1076:1077]; m2 = b.poolf[:, 1077:1078]
    den = b.poolf[:, 1078:1079]; rden = b.poolf[:, 1079:1080]
    cbf = b.pool[:, RCB:RCB + 8]
    xnf = b.poolf[:, 0:1024]
    P.w("v", "qg", qg_small)
    P.w("v", "qs", qs_sm)
    ve_wl = None
    for dc in range(ND):
        ve_wl = P.tt("v", b.pool[:, RWG2 + dc * 8: RWG2 + (dc + 1) * 8],
                     wgf[:, dc * 8:(dc + 1) * 8], b.pool[:, RWG + dc * 8: RWG + (dc + 1) * 8],
                     ALU.subtract, inc=True)
    P.w("t", "ve", ve_wl)
    ve_c = None
    te_ct_evac_war = None
    te_xl_war = None
    for tt in range(NT):
        vt = P.ts("v", xnf, b.x[:, tt * DM:(tt + 1) * DM], b.st[:, 32 + tt:32 + tt + 1],
                  b.st[:, 48 + tt:48 + tt + 1], ALU.mult, ALU.add, inc=True)
        P.w("v", "ve", vt)
        vt = P.cp("v", b.sqs[:], xnf, inc=True)
        P.w("v", "ve", vt)
        vt = P.tt("v", b.xn[:, 0:DM], xnf, b.sqs[:], ALU.subtract, inc=True)
        P.w("t", "ve", vt)
        if te_xl_war is not None:
            P.w("t", "ve", te_xl_war)
        for dc in range(ND):
            te_t = P.tr(b.ptb[:, dc * 128:(dc + 1) * 128], b.xn[:, dc * 128:(dc + 1) * 128],
                        b.ident[:], inc=(dc == ND - 1))
        P.w("v", "te", te_t)
        xlt_view = b.pool[:, RXL:RXL + 1024].rearrange("p (dc j) -> p dc j", dc=ND)
        ptb_view = b.ptb[:].rearrange("p (dc j) -> p dc j", dc=ND)
        ve_xl = P.cp("v", xlt_view, ptb_view, inc=True)
        P.w("t", "ve", ve_xl)
        ps = b.psc[:, (tt % 2) * 512:(tt % 2) * 512 + 8]
        n_mm = 0
        for dc in range(ND):
            lhs_h = xnt_chunk(b, dc, tt * 128, (tt + 1) * 128)
            lhs_l = b.pool[:, RXL + dc * 128: RXL + (dc + 1) * 128]
            rhs_h = b.pool[:, RWG + dc * 8: RWG + (dc + 1) * 8]
            rhs_l = b.pool[:, RWG2 + dc * 8: RWG2 + (dc + 1) * 8]
            for (lh, rh) in ((lhs_h, rhs_h), (lhs_h, rhs_l), (lhs_l, rhs_h)):
                te_t = P.mm(ps, lh, rh, start=(n_mm == 0), stop=(n_mm == 23),
                            inc=(n_mm == 23))
                n_mm += 1
        te_xl_war = te_t
        P.w("v", "te", te_t)

        def ve_step(fn):
            tgt = fn()
            P.w("v", "ve", tgt)
            return tgt

        ve_step(lambda: P.tt("v", lg, ps, b.smf[:, BGB:BGB + 8], ALU.add, inc=True))
        ve_step(lambda: P.red("v", rmax, lg, ALU.max, inc=True))
        nrm_t = ve_step(lambda: P.ts("v", nrm, rmax, -1.0, None, ALU.mult, inc=True))
        P.w("s", "ve", nrm_t)
        se_t = P.act(ex, lg, AF.Exp, bias=nrm, accum=esum, inc=True)
        P.w("v", "se", se_t)
        ve_step(lambda: P.recip(res_, esum, inc=True))
        ve_step(lambda: P.ts("v", pp, ex, res_, None, ALU.mult, inc=True))
        ve_step(lambda: P.red("v", m1, pp, ALU.max, inc=True))
        ve_step(lambda: P.ts("v", tmp, pp, m1, None, ALU.is_equal, inc=True))
        ve_step(lambda: P.tt("v", tmp, pp, tmp, ALU.mult, inc=True))
        ve_step(lambda: P.tt("v", pm, pp, tmp, ALU.subtract, inc=True))
        ve_step(lambda: P.red("v", m2, pm, ALU.max, inc=True))
        ve_step(lambda: P.ts("v", tmp, pp, m2, None, ALU.is_ge, inc=True))
        ve_step(lambda: P.tt("v", cu, pp, tmp, ALU.mult, inc=True))
        ve_step(lambda: P.red("v", den, cu, ALU.add, inc=True))
        ve_step(lambda: P.recip(rden, den, inc=True))
        ve_c = ve_step(lambda: P.ts("v", b.smf[:, CMF + tt * 8:CMF + (tt + 1) * 8],
                                    cu, rden, None, ALU.mult, inc=True))
        ve_cb = ve_step(lambda: P.cp("v", cbf, b.smf[:, CMF + tt * 8:CMF + (tt + 1) * 8],
                                     inc=True))
        P.w("t", "ve", ve_cb)
        if te_ct_evac_war is not None and tt % 8 == 0 and tt > 0:
            P.w("t", "ve", te_ct_evac_war)
        te_tr = P.tr(b.ptb[0:8, (tt % 8) * 128:(tt % 8 + 1) * 128], cbf, b.ident[:], inc=True)
        P.w("v", "te", te_tr)
        te_ct_evac_war = P.cp("v", b.ct[:, tt * 128:(tt + 1) * 128],
                              b.ptb[0:8, (tt % 8) * 128:(tt % 8 + 1) * 128], inc=True)
    ve_ct = te_ct_evac_war
    ve_gate = ve_c

    P.w("t", "ve", ve_ct)
    ve_b2_slot = {}
    for tt in range(NT):
        for Dg in range(2):
            sl = (tt * 2 + Dg) % 2
            ps = b.psa[:, sl * 512:(sl + 1) * 512]
            if sl in ve_b2_slot:
                P.w("t", "ve", ve_b2_slot[sl])
            te_t = P.mm(ps, b.ct[:, tt * 128:(tt + 1) * 128], b.b2e[:, Dg * 512:(Dg + 1) * 512],
                        inc=True)
            P.w("v", "te", te_t)
            xsl = b.x[:, tt * DM + Dg * 512: tt * DM + (Dg + 1) * 512]
            ve_b2_slot[sl] = P.tt("v", xsl, ps, xsl, ALU.add, inc=True)
    ve_x = ve_b2_slot[sl]

    te_w1_war = [None, None]
    te_w2_war = None
    se_h_slot = {}
    ve_y_slot = {}
    for e in range(8):
        for hf in range(2):
            par = (e * 2 + hf) % 2
            reg1 = RW1[par]
            if te_w1_war[par] is not None:
                P.w("g", "te", te_w1_war[par])
            t1 = None
            for dc in range(ND):
                t1 = P.dma("g", b.pool[:, reg1 + dc * 1024: reg1 + (dc + 1) * 1024],
                           dr["mw1"][e][dc * 128:(dc + 1) * 128, hf * 1024:(hf + 1) * 1024])
            if te_w2_war is not None:
                P.w("g", "te", te_w2_war)
            t2 = None
            for hc in range(8):
                t2 = P.dma("g", b.pool[:, RW2 + hc * 1024: RW2 + (hc + 1) * 1024],
                           dr["mw2"][e][(hf * 8 + hc) * 128:(hf * 8 + hc + 1) * 128, :])
            P.w("t", "qg", t1)
            te_h_last = None
            te_y_last = None
            for tg in range(8):
                hreg = RH[tg % 2]
                for hc in range(8):
                    sl = hc % 2
                    ps = b.psc[:, sl * 512: sl * 512 + 256]
                    if (sl, "h") in se_h_slot:
                        P.w("t", "se", se_h_slot[(sl, "h")])
                    for dc in range(ND):
                        lhs = b.pool[:, reg1 + dc * 1024 + hc * 128: reg1 + dc * 1024 + (hc + 1) * 128]
                        te_t = P.mm(ps, lhs, xnt_chunk(b, dc, tg * 256, (tg + 1) * 256),
                                    start=(dc == 0), stop=(dc == ND - 1), inc=(dc == ND - 1))
                    te_h_last = te_t
                    P.w("s", "te", te_t)
                    bcol = B1C + e * 16 + hf * 8 + hc
                    se_t = P.act(b.pool[:, hreg + hc * 256: hreg + (hc + 1) * 256], ps, AF.Relu,
                                 bias=b.smf[:, bcol:bcol + 1], inc=True)
                    se_h_slot[(sl, "h")] = se_t
                P.w("t", "se", se_t)
                P.w("t", "qg", t2) if tg == 0 else None
                for tsub in range(2):
                    tt_tok = tg * 2 + tsub
                    for Dg in range(2):
                        sl = (tsub * 2 + Dg) % 2
                        ps = b.psb[:, sl * 512:(sl + 1) * 512]
                        if (sl, "y") in ve_y_slot:
                            P.w("t", "ve", ve_y_slot[(sl, "y")])
                        for hc in range(8):
                            lhs = b.pool[:, hreg + hc * 256 + tsub * 128: hreg + hc * 256 + (tsub + 1) * 128]
                            rhs = b.pool[:, RW2 + hc * 1024 + Dg * 512: RW2 + hc * 1024 + (Dg + 1) * 512]
                            te_t = P.mm(ps, lhs, rhs, start=(hc == 0), stop=(hc == 7),
                                        inc=(hc == 7))
                        te_y_last = te_t
                        P.w("v", "te", te_t)
                        xsl = b.x[:, tt_tok * DM + Dg * 512: tt_tok * DM + (Dg + 1) * 512]
                        ccol = CMF + tt_tok * 8 + e
                        ve_t = P.stt("v", xsl, ps, b.smf[:, ccol:ccol + 1], xsl,
                                     ALU.mult, ALU.add, inc=True)
                        ve_y_slot[(sl, "y")] = ve_t
            te_w1_war[par] = te_h_last
            te_w2_war = te_y_last
            ve_x = ve_t
    return ve_x




ATTN_PAIRS = 8
ATTN_QTPS = 8

def emit_attn_a(P, b, dr, xnt_ve, causal, rope, wq_slice, wk_slice, wv_slice, wo_slice,
                bout=False):
    pool = ColAlloc(POOL_COLS)
    RQ = pool.take(2048)
    RK = pool.take(2048)
    RV = pool.take(2048)
    RABS = pool.take(2048)
    RP = pool.take(8192)
    RPT = pool.take(8192)
    RDG = pool.take(256)
    ROT = pool.take(512)
    RW = pool.take(3072)
    RWO = pool.take(1024)
    RMK = pool.take(16)

    NB = 241; RS = 243; RSUM = 247; IVS = 249

    ve_x = None
    te_w_war = None
    war_sc = [None, None]
    war_pv = None
    war_y = {0: None, 1: None}
    war_pt32 = None
    war_rot = [None, None]
    war_p = {}
    war_dg = [None, None]

    def bw(e, war):
        if war is not None:
            P.w(e, war[0], war[1])

    for pp in range(ATTN_PAIRS):
        if te_w_war is not None:
            P.w("g", "te", te_w_war)
        for dc in range(ND):
            P.dma("g", b.pool[:, RW + dc * 384: RW + dc * 384 + 128], wq_slice(dr, dc, pp))
            P.dma("g", b.pool[:, RW + dc * 384 + 128: RW + dc * 384 + 256], wk_slice(dr, dc, pp))
            t1 = P.dma("g", b.pool[:, RW + dc * 384 + 256: RW + dc * 384 + 384], wv_slice(dr, dc, pp))
        t2 = P.dma("g", b.pool[:, RWO:RWO + 1024], wo_slice(dr, pp))
        P.w("t", "qg", t1)
        if pp == 0:
            P.w("t", "ve", xnt_ve)

        ve_qk = None
        for (reg, woff) in ((RQ, 0), (RK, 128)):
            for tg in range(4):
                bw("t", war_sc[0])
                ps = b.psa[:, (tg % 2) * 512:(tg % 2) * 512 + 512]
                for dc in range(ND):
                    lhs = b.pool[:, RW + dc * 384 + woff: RW + dc * 384 + woff + 128]
                    te_t = P.mm(ps, lhs, xnt_chunk(b, dc, tg * 512, (tg + 1) * 512),
                                start=(dc == 0), stop=(dc == ND - 1), inc=(dc == ND - 1))
                P.w("v", "te", te_t)
                ve_qk = P.cp("v", b.pool[:, reg + tg * 512: reg + (tg + 1) * 512], ps, inc=True)
                war_sc[0] = ("ve", ve_qk)

        ve_v = None
        for tt in range(NT):
            bw("t", war_sc[1])
            ps = b.psb[:, (tt % 2) * 512:(tt % 2) * 512 + 128]
            for dc in range(ND):
                lhs = xnt_chunk(b, dc, tt * 128, (tt + 1) * 128)
                rhs = b.pool[:, RW + dc * 384 + 256: RW + dc * 384 + 384]
                te_t = P.mm(ps, lhs, rhs, start=(dc == 0), stop=(dc == ND - 1),
                            inc=(dc == ND - 1))
            P.w("v", "te", te_t)
            ve_v = P.cp("v", b.pool[:, RV + tt * 128: RV + (tt + 1) * 128], ps, inc=True)
            war_sc[1] = ("ve", ve_v)

        if rope:
            ve_rot = ve_v
            for reg in (RQ, RK):
                for half in range(2):
                    c0 = half * 1024
                    bw("t", war_sc[0])
                    P.w("t", "ve", ve_rot)
                    ps = b.psa[:, 0:1024]
                    for j in range(2):
                        te_t = P.mm(ps[:, j * 512:(j + 1) * 512], b.swap[:],
                                    b.pool[:, reg + c0 + j * 512: reg + c0 + (j + 1) * 512],
                                    inc=(j == 1))
                    P.w("v", "te", te_t)
                    tmp = b.poolf[:, 0:1024]
                    t2f = b.poolf[:, 1024:2048]
                    vt = P.tt("v", tmp, ps, b.sin[:, c0:c0 + 1024], ALU.mult, inc=True)
                    war_sc[0] = ("ve", vt)
                    vt2 = P.tt("v", t2f, b.pool[:, reg + c0:reg + c0 + 1024],
                               b.cos[:, c0:c0 + 1024], ALU.mult, inc=True)
                    P.w("v", "ve", vt2)
                    ve_rot = P.tt("v", b.pool[:, reg + c0:reg + c0 + 1024], tmp, t2f,
                                  ALU.add, inc=True)
            ve_qk_done = ve_rot
        else:
            ve_qk_done = ve_qk

        P.w("v", "ve", ve_qk_done)
        ve_mk = P.red("v", b.pool[:, RMK:RMK + 1], b.pool[:, RK:RK + 2048], ALU.max,
                      absval=True, inc=True)
        P.w("v", "ve", ve_mk)
        P.w("s", "ve", ve_qk_done)
        se_absq = P.act(b.pool[:, RABS:RABS + 2048], b.pool[:, RQ:RQ + 2048], AF.Abs, inc=True)
        P.w("t", "se", se_absq)
        P.w("t", "ve", ve_mk)
        war_sc[0] = war_sc[0] or ("ve", ve_qk_done)
        war_sc[1] = war_sc[1] or ("ve", ve_qk_done)

        sr_i = 0
        for qtp in range(ATTN_QTPS):
            qts = [2 * qtp, 2 * qtp + 1]
            ve_pt = None
            for par, qt in enumerate(qts):
                ext = (qt + 1) * 128 if causal else N
                hsubs = (0, 64)
                bw("t", war_pt32)
                bw("t", war_y[1])
                for hi, hb in enumerate(hsubs):
                    te_t = P.mm(b.pt32[:, hi:hi + 1],
                                b.pool[hb:hb + 64, RABS + qt * 128: RABS + (qt + 1) * 128],
                                b.pool[hb:hb + 64, RMK:RMK + 1], inc=(hi == 1))
                P.w("v", "te", te_t)
                vt = P.ts("v", b.smf[:, NB:NB + 2], b.pt32[:, 0:2], -0.125, None,
                          ALU.mult, inc=True)
                P.w("v", "ve", vt)
                P.w("s", "ve", vt)
                war_pt32 = ("ve", vt)
                for hi, hb in enumerate(hsubs):
                    preg = RP + hi * 4096 + par * 2048
                    if (hi, par) in war_p:
                        bw("s", war_p[(hi, par)])
                    nkh = (ext + 1023) // 1024
                    accs = []
                    for kh in range(nkh):
                        w = min(1024, ext - kh * 1024)
                        ri = sr_i % 2; sr_i += 1
                        reg = (b.psa, b.psb)[ri]
                        bw("t", war_sc[ri])
                        nkg = (w + 511) // 512
                        for kg in range(nkg):
                            w2 = min(512, w - kg * 512)
                            te_t = P.mm(reg[:, kg * 512:kg * 512 + w2],
                                        b.pool[hb:hb + 64, RQ + qt * 128: RQ + (qt + 1) * 128],
                                        b.pool[hb:hb + 64, RK + kh * 1024 + kg * 512:
                                               RK + kh * 1024 + kg * 512 + w2],
                                        inc=(kg == nkg - 1))
                        if causal and kh == nkh - 1:
                            P.w("v", "te", te_t)
                            off = qt * 128 - kh * 1024
                            vt = P.tt("v", reg[:, off:off + 128], reg[:, off:off + 128],
                                      b.cmask[:], ALU.add, inc=True)
                            P.w("s", "ve", vt)
                        else:
                            P.w("s", "te", te_t)
                        acc = b.smf[:, RS + kh:RS + kh + 1]
                        se_t = P.act(b.pool[:, preg + kh * 1024: preg + kh * 1024 + w],
                                     reg[:, 0:w], AF.Exp, scale=cpc(b, "scale_attn"),
                                     bias=b.smf[:, NB + hi:NB + hi + 1], accum=acc, inc=True)
                        war_sc[ri] = ("se", se_t)
                        accs.append(acc)
                    P.w("v", "se", se_t)
                    if len(accs) == 2:
                        vt = P.tt("v", b.smf[:, RSUM + hi:RSUM + hi + 1], accs[0], accs[1],
                                  ALU.add, inc=True)
                    else:
                        vt = P.cp("v", b.smf[:, RSUM + hi:RSUM + hi + 1], accs[0], inc=True)
                    P.w("v", "ve", vt)
                    vt = P.recip(b.smf[:, IVS + hi:IVS + hi + 1],
                                 b.smf[:, RSUM + hi:RSUM + hi + 1], inc=True)
                    P.w("v", "ve", vt)
                    bw("v", war_dg[hi])
                    ve_diag = P.ts("v", b.pool[:, RDG + hi * 128: RDG + (hi + 1) * 128],
                                   b.ident[:], b.smf[:, IVS + hi:IVS + hi + 1], None,
                                   ALU.mult, inc=True)
                    P.w("t", "ve", ve_diag)
                    nkc = ext // 128
                    for kc0 in range(0, nkc, 4):
                        kcs = list(range(kc0, min(kc0 + 4, nkc)))
                        bw("t", war_pt32)
                        for kc in kcs:
                            te_t = P.mm(b.pt32[:, (kc % 4) * 128:(kc % 4 + 1) * 128],
                                        b.pool[:, preg + kc * 128: preg + (kc + 1) * 128],
                                        b.pool[:, RDG + hi * 128: RDG + (hi + 1) * 128],
                                        inc=(kc == kcs[-1]))
                        P.w("v", "te", te_t)
                        nk = len(kcs)
                        base = RPT + hi * 4096 + kc0 * 256 + par * 128
                        dstv = b.pool[:, base: base + nk * 256].rearrange(
                            "p (kc j) -> p kc j", kc=nk)[:, :, 0:128]
                        srcv = b.pt32[:, 0:nk * 128].rearrange("p (kc j) -> p kc j", kc=nk)
                        ve_pt = P.cp("v", dstv, srcv, inc=True)
                        war_pt32 = ("ve", ve_pt)
                    war_p[(hi, par)] = ("te", te_t)
                    war_dg[hi] = ("te", te_t)
            if causal:
                ext0 = (qts[0] + 1) * 128
                ext1 = (qts[1] + 1) * 128
                vt = None
                for hi in range(2):
                    for kc in range(ext0 // 128, ext1 // 128):
                        base = RPT + hi * 4096 + kc * 256
                        vt = P.i("v", lambda eng, ap=b.pool[:, base:base + 128]:
                                 eng.memset(ap, 0), inc="ve")
                if vt is not None:
                    P.w("t", "ve", vt)
            P.w("t", "ve", ve_pt)

            nkc = ((qts[1] + 1) * 128 if causal else N) // 128
            pvps = b.psc[:, 0:256]
            bw("t", war_pv)
            bw("t", war_y[0])
            for hi in range(2):
                outap = pvps[0:64, :] if hi == 0 else pvps[64:128, :]
                for kc in range(nkc):
                    te_t = P.mm(outap,
                                b.pool[:, RV + kc * 128 + hi * 64: RV + kc * 128 + hi * 64 + 64],
                                b.pool[:, RPT + hi * 4096 + kc * 256: RPT + hi * 4096 + (kc + 1) * 256],
                                start=(kc == 0), stop=(kc == nkc - 1),
                                tp=None if hi == 0 else (0, 64), skip=(hi == 1),
                                inc=(kc == nkc - 1))
            P.w("v", "te", te_t)
            rotb = b.pool[:, ROT + (qtp % 2) * 256: ROT + (qtp % 2) * 256 + 256]
            bw("v", war_rot[qtp % 2])
            ve_ot = P.cp("v", rotb, pvps, inc=True)
            war_pv = ("ve", ve_ot)
            P.w("t", "ve", ve_ot)

            te_y_last = None
            for tsub in range(2):
                tt_tok = qtp * 2 + tsub
                for Dg in range(2):
                    sl = (tsub * 2 + Dg) % 2
                    ps = b.psc[:, 512:1024] if sl == 0 else b.pt32[:, 0:512]
                    bw("t", war_y[sl])
                    te_t = P.mm(ps, rotb[:, tsub * 128:(tsub + 1) * 128],
                                b.pool[:, RWO + Dg * 512: RWO + (Dg + 1) * 512], inc=True)
                    te_y_last = te_t
                    P.w("v", "te", te_t)
                    xsl = b.x[:, tt_tok * DM + Dg * 512: tt_tok * DM + (Dg + 1) * 512]
                    ve_t = P.tt("v", xsl, ps, xsl, ALU.add, inc=True)
                    war_y[sl] = ("ve", ve_t)
            war_rot[qtp % 2] = ("te", te_y_last)
            te_w_war = te_y_last
            ve_x = ve_t
    if bout:
        P.w("v", "qs", P.cnt["qs"])
        for tt in range(NT):
            xsl = b.x[:, tt * DM:(tt + 1) * DM]
            ve_x = P.tt("v", xsl, xsl, b.bout[:], ALU.add, inc=True)
    return ve_x




def emit_l1ab(P, b, dr, xnt_ve):
    pool = ColAlloc(POOL_COLS)
    RQ = pool.take(2048)
    RK = pool.take(2048)
    RV = pool.take(2048)
    RABS = pool.take(2048)
    RP = pool.take(8192)
    RPT = pool.take(4096)
    RDG = pool.take(256)
    ROT = pool.take(512)
    RW = pool.take(3072)
    RWO = pool.take(1024)
    RMK = pool.take(16)
    RLP = pool.take(64)

    NB = 241; RS = 243; RSUM = 247; IVS = 249
    LAMBC = 208
    RMS02 = 224
    IVR = 226
    SSQ = 228
    LQ = 256

    t = P.dma("y", b.smf[0:64, LQ:LQ + 16], dr["dlq1T"])
    t = P.dma("y", b.smf[0:64, LQ + 16:LQ + 32], dr["dlk1T"])
    t = P.dma("y", b.smf[0:64, LQ + 32:LQ + 48], dr["dlq2T"])
    t = P.dma("y", b.smf[0:64, LQ + 48:LQ + 64], dr["dlk2T"])
    t = P.dma("y", b.smf[:, RMS02:RMS02 + 1], dr["drms"])
    qs_sm = t

    P.w("v", "qs", qs_sm)
    vt = P.ts("v", b.smf[:, RMS02:RMS02 + 1], b.smf[:, RMS02:RMS02 + 1], 0.2, None,
              ALU.mult, inc=True)
    P.w("v", "ve", vt)
    ve_rms02 = vt
    lamrow = b.poolf[0:1, 0:16]
    e2row = b.poolf[0:1, 16:32]
    for i in range(2):
        vt = P.tt("v", b.pool[0:64, RLP:RLP + 16], b.smf[0:64, LQ + i * 32:LQ + i * 32 + 16],
                  b.smf[0:64, LQ + i * 32 + 16:LQ + i * 32 + 32], ALU.mult, inc=True)
        P.w("t", "ve", vt)
        te_t = P.mm(b.pt32[0:1, 0:16], b.ones[0:64, :], b.pool[0:64, RLP:RLP + 16], inc=True)
        P.w("s", "te", te_t)
        se_t = P.act(lamrow if i == 0 else e2row, b.pt32[0:1, 0:16], AF.Exp, inc=True)
        P.w("v", "se", se_t)
        P.w("t", "se", se_t)
    vt = P.tt("v", lamrow, lamrow, e2row, ALU.subtract, inc=True)
    P.w("v", "ve", vt)
    vt = P.ts("v", lamrow, lamrow, 0.8, None, ALU.add, inc=True)
    P.w("v", "ve", vt)
    lamb = b.pool[0:1, RLP + 16:RLP + 32]
    vt = P.cp("v", lamb, lamrow, inc=True)
    P.w("t", "ve", vt)
    for h in range(16):
        te_t = P.mm(b.pt32[:, 64 + h:64 + h + 1], b.onesr[:], lamb[:, h:h + 1],
                    inc=(h == 15))
    P.w("v", "te", te_t)
    ve_lam = P.cp("v", b.smf[:, LAMBC:LAMBC + 16], b.pt32[:, 64:80], inc=True)

    ve_x = None
    te_w_war = None
    war_sc = [None, None]
    war_pv = None
    war_y = {0: None, 1: None}
    war_pt32 = ("ve", ve_lam)
    war_rot = [None, None]
    war_p = {}
    war_dg = [None, None]
    war_ptb = None

    def bw(e, war):
        if war is not None:
            P.w(e, war[0], war[1])

    for h in range(16):
        if te_w_war is not None:
            P.w("g", "te", te_w_war)
        for dc in range(ND):
            P.dma("g", b.pool[:, RW + dc * 384: RW + dc * 384 + 128],
                  dr["bwqT"][dc * 128:(dc + 1) * 128, h * 128:(h + 1) * 128])
            P.dma("g", b.pool[:, RW + dc * 384 + 128: RW + dc * 384 + 256],
                  dr["bwkT"][dc * 128:(dc + 1) * 128, h * 128:(h + 1) * 128])
            t1 = P.dma("g", b.pool[:, RW + dc * 384 + 256: RW + dc * 384 + 384],
                       dr["bwvT"][dc * 128:(dc + 1) * 128, h * 128:(h + 1) * 128])
        t2 = P.dma("g", b.pool[:, RWO:RWO + 1024], dr["bwoT"][h * 128:(h + 1) * 128, :])
        P.w("v", "qg", t2)
        P.w("v", "ve", ve_rms02) if h == 0 else None
        ve_wo = P.ts("v", b.pool[:, RWO:RWO + 1024], b.pool[:, RWO:RWO + 1024],
                     b.smf[:, RMS02:RMS02 + 1], None, ALU.mult, inc=True)
        P.w("t", "qg", t1)
        if h == 0:
            P.w("t", "ve", xnt_ve)

        ve_qk = None
        for (reg, woff) in ((RQ, 0), (RK, 128)):
            for tg in range(4):
                bw("t", war_sc[0])
                ps = b.psa[:, (tg % 2) * 512:(tg % 2) * 512 + 512]
                for dc in range(ND):
                    lhs = b.pool[:, RW + dc * 384 + woff: RW + dc * 384 + woff + 128]
                    te_t = P.mm(ps, lhs, xnt_chunk(b, dc, tg * 512, (tg + 1) * 512),
                                start=(dc == 0), stop=(dc == ND - 1), inc=(dc == ND - 1))
                P.w("v", "te", te_t)
                ve_qk = P.cp("v", b.pool[:, reg + tg * 512: reg + (tg + 1) * 512], ps, inc=True)
                war_sc[0] = ("ve", ve_qk)

        ve_v = None
        for tt in range(NT):
            bw("t", war_sc[1])
            ps = b.psb[:, (tt % 2) * 512:(tt % 2) * 512 + 128]
            for dc in range(ND):
                lhs = xnt_chunk(b, dc, tt * 128, (tt + 1) * 128)
                rhs = b.pool[:, RW + dc * 384 + 256: RW + dc * 384 + 384]
                te_t = P.mm(ps, lhs, rhs, start=(dc == 0), stop=(dc == ND - 1),
                            inc=(dc == ND - 1))
            P.w("v", "te", te_t)
            ve_v = P.cp("v", b.pool[:, RV + tt * 128: RV + (tt + 1) * 128], ps, inc=True)
            war_sc[1] = ("ve", ve_v)

        P.w("v", "ve", ve_v)
        ve_mk = P.red("v", b.pool[:, RMK:RMK + 1], b.pool[:, RK:RK + 2048], ALU.max,
                      absval=True, inc=True)
        P.w("v", "ve", ve_mk)
        P.w("s", "ve", ve_v)
        se_absq = P.act(b.pool[:, RABS:RABS + 2048], b.pool[:, RQ:RQ + 2048], AF.Abs, inc=True)
        P.w("t", "se", se_absq)
        P.w("t", "ve", ve_mk)
        war_sc[0] = war_sc[0] or ("ve", ve_mk)
        war_sc[1] = war_sc[1] or ("ve", ve_mk)

        sr_i = 0
        for qtp in range(8):
            qts = [2 * qtp, 2 * qtp + 1]
            ve_pt = None
            for par, qt in enumerate(qts):
                ext = (qt + 1) * 128
                hsubs = (0, 64)
                bw("t", war_pt32)
                bw("t", war_y[1])
                for hi, hb in enumerate(hsubs):
                    te_t = P.mm(b.pt32[:, hi:hi + 1],
                                b.pool[hb:hb + 64, RABS + qt * 128: RABS + (qt + 1) * 128],
                                b.pool[hb:hb + 64, RMK:RMK + 1], inc=(hi == 1))
                P.w("v", "te", te_t)
                vt = P.ts("v", b.smf[:, NB:NB + 2], b.pt32[:, 0:2], -0.125, None,
                          ALU.mult, inc=True)
                P.w("v", "ve", vt)
                P.w("s", "ve", vt)
                war_pt32 = ("ve", vt)
                for hi, hb in enumerate(hsubs):
                    preg = RP + hi * 4096 + par * 2048
                    if (hi, par) in war_p:
                        bw("s", war_p[(hi, par)])
                    nkh = (ext + 1023) // 1024
                    accs = []
                    for kh in range(nkh):
                        w = min(1024, ext - kh * 1024)
                        ri = sr_i % 2; sr_i += 1
                        reg = (b.psa, b.psb)[ri]
                        bw("t", war_sc[ri])
                        nkg = (w + 511) // 512
                        for kg in range(nkg):
                            w2 = min(512, w - kg * 512)
                            te_t = P.mm(reg[:, kg * 512:kg * 512 + w2],
                                        b.pool[hb:hb + 64, RQ + qt * 128: RQ + (qt + 1) * 128],
                                        b.pool[hb:hb + 64, RK + kh * 1024 + kg * 512:
                                               RK + kh * 1024 + kg * 512 + w2],
                                        inc=(kg == nkg - 1))
                        if kh == nkh - 1:
                            P.w("v", "te", te_t)
                            off = qt * 128 - kh * 1024
                            vt = P.tt("v", reg[:, off:off + 128], reg[:, off:off + 128],
                                      b.cmask[:], ALU.add, inc=True)
                            P.w("s", "ve", vt)
                        else:
                            P.w("s", "te", te_t)
                        acc = b.smf[:, RS + 2 * hi + kh:RS + 2 * hi + kh + 1]
                        se_t = P.act(b.pool[:, preg + kh * 1024: preg + kh * 1024 + w],
                                     reg[:, 0:w], AF.Exp, scale=cpc(b, "scale_attn"),
                                     bias=b.smf[:, NB + hi:NB + hi + 1], accum=acc, inc=True)
                        war_sc[ri] = ("se", se_t)
                        accs.append(acc)
                    P.w("v", "se", se_t)
                    if len(accs) == 2:
                        vt = P.tt("v", b.smf[:, RSUM + hi:RSUM + hi + 1], accs[0], accs[1],
                                  ALU.add, inc=True)
                    else:
                        vt = P.cp("v", b.smf[:, RSUM + hi:RSUM + hi + 1], accs[0], inc=True)
                    P.w("v", "ve", vt)
                    vt = P.recip(b.smf[:, IVS + hi:IVS + hi + 1],
                                 b.smf[:, RSUM + hi:RSUM + hi + 1], inc=True)
                    P.w("v", "ve", vt)
                    if hi == 1:
                        vt = P.ts("v", b.smf[:, IVS + 1:IVS + 2], b.smf[:, IVS + 1:IVS + 2],
                                  b.smf[:, LAMBC + h:LAMBC + h + 1], -1.0, ALU.mult, ALU.mult,
                                  inc=True)
                        P.w("v", "ve", vt)
                    bw("v", war_dg[hi])
                    ve_diag = P.ts("v", b.pool[:, RDG + hi * 128: RDG + (hi + 1) * 128],
                                   b.ident[:], b.smf[:, IVS + hi:IVS + hi + 1], None,
                                   ALU.mult, inc=True)
                    P.w("t", "ve", ve_diag)
                nkc = ext // 128
                for kc0 in range(0, nkc, 4):
                    kcs = list(range(kc0, min(kc0 + 4, nkc)))
                    bw("t", war_pt32)
                    for kc in kcs:
                        for hi in range(2):
                            preg = RP + hi * 4096 + par * 2048
                            te_t = P.mm(b.pt32[:, (kc % 4) * 128:(kc % 4 + 1) * 128],
                                        b.pool[:, preg + kc * 128: preg + (kc + 1) * 128],
                                        b.pool[:, RDG + hi * 128: RDG + (hi + 1) * 128],
                                        start=(hi == 0), stop=(hi == 1),
                                        inc=(kc == kcs[-1] and hi == 1))
                    P.w("v", "te", te_t)
                    nk = len(kcs)
                    base = RPT + kc0 * 256 + par * 128
                    dstv = b.pool[:, base: base + nk * 256].rearrange(
                        "p (kc j) -> p kc j", kc=nk)[:, :, 0:128]
                    srcv = b.pt32[:, 0:nk * 128].rearrange("p (kc j) -> p kc j", kc=nk)
                    ve_pt = P.cp("v", dstv, srcv, inc=True)
                    war_pt32 = ("ve", ve_pt)
                war_p[(0, par)] = ("te", te_t)
                war_p[(1, par)] = ("te", te_t)
                war_dg[0] = ("te", te_t)
                war_dg[1] = ("te", te_t)
            ext0 = (qts[0] + 1) * 128
            ext1 = (qts[1] + 1) * 128
            vt = None
            for kc in range(ext0 // 128, ext1 // 128):
                base = RPT + kc * 256
                vt = P.i("v", lambda eng, ap=b.pool[:, base:base + 128]:
                         eng.memset(ap, 0), inc="ve")
            if vt is not None:
                P.w("t", "ve", vt)
            P.w("t", "ve", ve_pt)

            nkc = ext1 // 128
            pvps = b.psc[:, 0:256]
            bw("t", war_pv)
            for kc in range(nkc):
                te_t = P.mm(pvps, b.pool[:, RV + kc * 128: RV + (kc + 1) * 128],
                            b.pool[:, RPT + kc * 256: RPT + (kc + 1) * 256],
                            start=(kc == 0), stop=(kc == nkc - 1), inc=(kc == nkc - 1))
            P.w("v", "te", te_t)
            rotb = b.pool[:, ROT + (qtp % 2) * 256: ROT + (qtp % 2) * 256 + 256]
            bw("v", war_rot[qtp % 2])
            ve_ot = P.cp("v", rotb, pvps, inc=True)
            war_pv = ("ve", ve_ot)
            P.w("t", "ve", ve_ot)

            bw("t", war_ptb)
            for tsub in range(2):
                te_t = P.tr(b.ptb[:, tsub * 128:(tsub + 1) * 128],
                            rotb[:, tsub * 128:(tsub + 1) * 128], b.ident[:],
                            inc=(tsub == 1))
            P.w("s", "te", te_t)
            for tsub in range(2):
                se_t = P.act(b.sqs[:, 0:128], b.ptb[:, tsub * 128:(tsub + 1) * 128],
                             AF.Square, accum=b.smf[:, SSQ + tsub:SSQ + tsub + 1],
                             inc=(tsub == 1))
            war_ptb = ("se", se_t)
            P.w("s", "se", se_t)
            se_t = P.act(b.smf[:, SSQ:SSQ + 2], b.smf[:, SSQ:SSQ + 2], AF.Sqrt,
                         bias=cpc(b, "eps"), scale=cpc(b, "inv_128"), inc=True)
            P.w("v", "se", se_t)
            vt = P.recip(b.smf[:, IVR:IVR + 2], b.smf[:, SSQ:SSQ + 2], inc=True)
            P.w("v", "ve", vt)
            ve_ivr = vt

            te_y_last = None
            P.w("t", "ve", ve_wo) if qtp == 0 else None
            for tsub in range(2):
                tt_tok = qtp * 2 + tsub
                for Dg in range(2):
                    sl = (tsub * 2 + Dg) % 2
                    ps = b.psc[:, 512:1024] if sl == 0 else b.pt32[:, 0:512]
                    bw("t", war_y[sl])
                    te_t = P.mm(ps, rotb[:, tsub * 128:(tsub + 1) * 128],
                                b.pool[:, RWO + Dg * 512: RWO + (Dg + 1) * 512], inc=True)
                    te_y_last = te_t
                    P.w("v", "te", te_t)
                    xsl = b.x[:, tt_tok * DM + Dg * 512: tt_tok * DM + (Dg + 1) * 512]
                    ve_t = P.stt("v", xsl, ps, b.smf[:, IVR + tsub:IVR + tsub + 1], xsl,
                                 ALU.mult, ALU.add, inc=True)
                    war_y[sl] = ("ve", ve_t)
            war_rot[qtp % 2] = ("te", te_y_last)
            ve_x = ve_t
        te_w_war = te_y_last
    return ve_x



_COMMON_NAMES = ["cpool", "ident_bf", "swap_bf", "ones_bf", "onesr_bf", "zeros_bf", "cmask", "rope_cos", "rope_sin"]


def _declare_dram(nc, st, parts):
    dr = {}
    def d(name, arr):
        dt = BF16 if arr.dtype == BF else F32
        dr[name] = nc.dram_tensor(name, list(arr.shape), dt, kind="ExternalInput").ap()
    for k in _COMMON_NAMES:
        d(k, st[k])
    if "l0ffn" in parts:
        d("w1T", st["l0ffn_w1T"]); d("w2T", st["l0ffn_w2T"])
        d("b1c", st["l0ffn_b1c"]); d("b2bc", st["l0ffn_b2bc"])
    if "l0ab" in parts:
        d("wthT", st["l0ab_wthT"]); d("wphT", st["l0ab_wphT"]); d("wgT", st["l0ab_wgT"])
    if "l1ffn" in parts:
        d("mwg", st["l1ffn_wg"]); d("mbg", st["l1ffn_bg_bc"]); d("mw1", st["l1ffn_w1"])
        d("mw2", st["l1ffn_w2"]); d("mb1c", st["l1ffn_b1c"]); d("mb2", st["l1ffn_b2"])
    if "l1aa" in parts:
        d("awqT", st["l1aa_wqT"]); d("awkT", st["l1aa_wkT"])
        d("awvT", st["l1aa_wvT"]); d("awoT", st["l1aa_woT"])
    if "l0aa" in parts:
        d("wqkv", st["l0aa_wqkv"]); d("wout", st["l0aa_wout"])
        d("gamma_bc", st["l0aa_gamma_bc"]); d("beta_bc", st["l0aa_beta_bc"])
        d("bout_bc", st["l0aa_bout_bc"])
    if "l1ab" in parts:
        d("bwqT", st["l1ab_wqT"]); d("bwkT", st["l1ab_wkT"])
        d("bwvT", st["l1ab_wvT"]); d("bwoT", st["l1ab_woT"])
        d("dlq1T", st["l1ab_lq1T"]); d("dlk1T", st["l1ab_lk1T"])
        d("dlq2T", st["l1ab_lq2T"]); d("dlk2T", st["l1ab_lk2T"])
        d("drms", st["l1ab_rms"])
    dr["x_in"] = nc.dram_tensor("x_in", [N, DM], F32, kind="ExternalInput").ap()
    dr["x_out"] = nc.dram_tensor("x_out", [N, DM], F32, kind="ExternalOutput").ap()
    if "lndump" in parts:
        dr["dbg_xnt"] = nc.dram_tensor("dbg_xnt", [128, ND * N], BF16, kind="ExternalOutput").ap()
        dr["dbg_st"] = nc.dram_tensor("dbg_st", [128, 192], F32, kind="ExternalOutput").ap()
    if "pooldump" in parts:
        dr["dbg_pool"] = nc.dram_tensor("dbg_pool", [128, POOL_COLS], BF16, kind="ExternalOutput").ap()
        dr["dbg_poolf"] = nc.dram_tensor("dbg_poolf", [128, POOLF_COLS], F32, kind="ExternalOutput").ap()
        dr["dbg_smf"] = nc.dram_tensor("dbg_smf", [128, 512], F32, kind="ExternalOutput").ap()
    return dr


def _in_map_common(st, parts):
    m = {k: st[k] for k in _COMMON_NAMES}
    if "l0ffn" in parts:
        m["w1T"] = st["l0ffn_w1T"]; m["w2T"] = st["l0ffn_w2T"]
        m["b1c"] = st["l0ffn_b1c"]; m["b2bc"] = st["l0ffn_b2bc"]
    if "l0ab" in parts:
        m["wthT"] = st["l0ab_wthT"]; m["wphT"] = st["l0ab_wphT"]; m["wgT"] = st["l0ab_wgT"]
    if "l1ffn" in parts:
        m["mwg"] = st["l1ffn_wg"]; m["mbg"] = st["l1ffn_bg_bc"]; m["mw1"] = st["l1ffn_w1"]
        m["mw2"] = st["l1ffn_w2"]; m["mb1c"] = st["l1ffn_b1c"]; m["mb2"] = st["l1ffn_b2"]
    if "l1aa" in parts:
        m["awqT"] = st["l1aa_wqT"]; m["awkT"] = st["l1aa_wkT"]
        m["awvT"] = st["l1aa_wvT"]; m["awoT"] = st["l1aa_woT"]
    if "l0aa" in parts:
        m["wqkv"] = st["l0aa_wqkv"]; m["wout"] = st["l0aa_wout"]
        m["gamma_bc"] = st["l0aa_gamma_bc"]; m["beta_bc"] = st["l0aa_beta_bc"]
        m["bout_bc"] = st["l0aa_bout_bc"]
    if "l1ab" in parts:
        m["bwqT"] = st["l1ab_wqT"]; m["bwkT"] = st["l1ab_wkT"]
        m["bwvT"] = st["l1ab_wvT"]; m["bwoT"] = st["l1ab_woT"]
        m["dlq1T"] = st["l1ab_lq1T"]; m["dlk1T"] = st["l1ab_lk1T"]
        m["dlq2T"] = st["l1ab_lq2T"]; m["dlk2T"] = st["l1ab_lk2T"]
        m["drms"] = st["l1ab_rms"]
    return m


def build_nc(st, parts):
    nc = bass.Bass("TRN2", target_bir_lowering=False)
    ctx = ExitStack()
    dr = _declare_dram(nc, st, parts)
    b = _alloc_bufs(nc, ctx)
    P = Prog(nc)

    for tt in range(NT):
        P.dma("y", b.x[:, tt * DM:(tt + 1) * DM], dr["x_in"][tt * 128:(tt + 1) * 128, :])
    P.dma("y", b.cp[:], dr["cpool"])
    P.dma("y", b.ident[:], dr["ident_bf"])
    P.dma("y", b.swap[:], dr["swap_bf"])
    P.dma("y", b.ones[:], dr["ones_bf"])
    P.dma("y", b.onesr[:], dr["onesr_bf"])
    P.dma("y", b.zeros[:], dr["zeros_bf"])
    P.dma("y", b.cmask[:], dr["cmask"])
    P.dma("y", b.cos[:], dr["rope_cos"])
    qs0 = P.dma("y", b.sin[:], dr["rope_sin"])
    for e in ("t", "v", "s"):
        P.w(e, "qs", qs0)

    ve_x = None
    for part in parts:
        if part == "pooldump":
            continue
        P.barrier()
        if part == "ln":
            ve_x = emit_ln_xnt(P, b, ve_x, gamma_beta=False)
        elif part == "lndump":
            xnt_ve = emit_ln_xnt(P, b, ve_x, gamma_beta=False)
            P.w("y", "ve", xnt_ve)
            tgt = P.dma("y", dr["dbg_xnt"], b.xnt[:])
            tgt = P.dma("y", dr["dbg_st"], b.st[:])
            P.w("y", "qs", tgt)
        elif part == "l0ffn":
            xnt_ve = emit_ln_xnt(P, b, ve_x, gamma_beta=False)
            ve_x = emit_l0ffn(P, b, dr, xnt_ve)
        elif part == "l0ab":
            xnt_ve = emit_ln_xnt(P, b, ve_x, gamma_beta=False)
            ve_x = emit_l0ab(P, b, dr, xnt_ve)
        elif part == "l1ffn":
            xnt_ve = emit_ln_xnt(P, b, ve_x, gamma_beta=False)
            ve_x = emit_l1ffn(P, b, dr, xnt_ve)
        elif part == "l1aa":
            xnt_ve = emit_ln_xnt(P, b, ve_x, gamma_beta=False)
            ve_x = emit_attn_a(
                P, b, dr, xnt_ve, causal=True, rope=False,
                wq_slice=lambda dr, dc, pp: dr["awqT"][dc * 128:(dc + 1) * 128, pp * 128:(pp + 1) * 128],
                wk_slice=lambda dr, dc, pp: dr["awkT"][dc * 128:(dc + 1) * 128, pp * 128:(pp + 1) * 128],
                wv_slice=lambda dr, dc, pp: dr["awvT"][dc * 128:(dc + 1) * 128, pp * 128:(pp + 1) * 128],
                wo_slice=lambda dr, pp: dr["awoT"][pp * 128:(pp + 1) * 128, :])
        elif part == "l1ab":
            xnt_ve = emit_ln_xnt(P, b, ve_x, gamma_beta=False)
            ve_x = emit_l1ab(P, b, dr, xnt_ve)
        elif part == "l0aa":
            t = P.dma("y", b.gamma[:], dr["gamma_bc"])
            t = P.dma("y", b.beta[:], dr["beta_bc"])
            t = P.dma("y", b.bout[:], dr["bout_bc"])
            P.w("v", "qs", t)
            xnt_ve = emit_ln_xnt(P, b, ve_x, gamma_beta=True)
            ve_x = emit_attn_a(
                P, b, dr, xnt_ve, causal=False, rope=True,
                wq_slice=lambda dr, dc, pp: dr["wqkv"][dc * 128:(dc + 1) * 128, pp * 384:pp * 384 + 128],
                wk_slice=lambda dr, dc, pp: dr["wqkv"][dc * 128:(dc + 1) * 128, pp * 384 + 128:pp * 384 + 256],
                wv_slice=lambda dr, dc, pp: dr["wqkv"][dc * 128:(dc + 1) * 128, pp * 384 + 256:pp * 384 + 384],
                wo_slice=lambda dr, pp: dr["wout"][pp * 128:(pp + 1) * 128, :], bout=True)
        else:
            raise NotImplementedError(part)
    if "pooldump" in parts:
        P.w("y", "ve", P.cnt["ve"])
        P.w("y", "se", P.cnt["se"])
        P.w("y", "te", P.cnt["te"])
        tgt = P.dma("y", dr["dbg_pool"], b.pool[:])
        tgt = P.dma("y", dr["dbg_smf"], b.smf[:])
        tgt = P.dma("y", dr["dbg_poolf"], b.poolf[:])
        P.w("y", "qs", tgt)

    P.w("y", "ve", ve_x)
    for tt in range(NT):
        tgt = P.dma("y", dr["x_out"][tt * 128:(tt + 1) * 128, :], b.x[:, tt * DM:(tt + 1) * DM])
    P.w("y", "qs", tgt)

    with nc.Block() as block:
        @block.tensor
        def _(eng):
            for fn in P.q["t"]:
                fn(eng)

        @block.vector
        def _(eng):
            for fn in P.q["v"]:
                fn(eng)

        @block.scalar
        def _(eng):
            for fn in P.q["s"]:
                fn(eng)

        @block.gpsimd
        def _(eng):
            for fn in P.q["g"]:
                fn(eng)

        @block.sync
        def _(eng):
            for fn in P.q["y"]:
                fn(eng)

    ctx.close()
    return nc, dr


def run_parts(inputs, parts, x_list):
    st = _stage_common(inputs)
    nc, dr = build_nc(st, parts)
    common = _in_map_common(st, parts)
    in_maps = []
    for c in range(len(x_list)):
        m = dict(common)
        m["x_in"] = np.ascontiguousarray(x_list[c], dtype=np.float32)
        in_maps.append(m)
    res = run_bass_kernel_spmd(nc, in_maps, core_ids=list(range(len(x_list))))
    if any(p in ("lndump", "pooldump") for p in parts):
        return res.results
    return [r["x_out"] for r in res.results]


def kernel(**inputs):
    x = np.asarray(inputs["L0_attn_a_x"], dtype=np.float32)
    xs = [x[c] for c in range(NCORES)]
    for part in ALL_PARTS:
        xs = run_parts(inputs, [part], xs)
    return np.stack(xs, axis=0).astype(np.float32)


# version 03
# speedup vs baseline: 1.0267x; 1.0267x over previous
import numpy as np
import ml_dtypes
from contextlib import ExitStack

import concourse.bass as bass
import concourse.mybir as mybir
from concourse.bass_utils import run_bass_kernel_spmd

F32 = mybir.dt.float32
BF16 = mybir.dt.bfloat16
AF = mybir.ActivationFunctionType
ALU = mybir.AluOpType
AX = mybir.AxisListType
BF = ml_dtypes.bfloat16

NCORES = 8
N = 2048
DM = 1024
NT = 16
ND = 8

ALL_PARTS = ["l0aa", "l0ffn", "l0ab", "l0ffn", "l1aa", "l1ffn", "l1ab", "l1ffn"]

CP = {
    "eps": 0, "scale_attn": 1, "inv_d": 2, "one_p_eps": 3, "eps_sq": 4,
    "inv_4096": 5, "inv_1024": 6, "inv_128": 7, "lam_init": 8, "fifth": 9, "inv_32": 10,
}
_CPN = 16
_CP_VALS = {
    "eps": 1e-5, "scale_attn": 0.125, "inv_d": 1.0 / 1024, "one_p_eps": 1.0 + 1e-5,
    "eps_sq": 1e-10, "inv_4096": 1.0 / 4096, "inv_1024": 1.0 / 1024, "inv_128": 1.0 / 128,
    "lam_init": 0.8, "fifth": 0.2, "inv_32": 1.0 / 32,
}



ENG_KEYS = ("t", "v", "s", "g", "y")
SEM_NAMES = ("te", "ve", "se", "pe", "qs", "qg", "qc")
_QUEUE_SEM = {"y": "qs", "g": "qg", "s": "qc"}


class Prog:
    def __init__(self, nc):
        self.nc = nc
        self.q = {e: [] for e in ENG_KEYS}
        self.sems = {}
        self.cnt = {s: 0 for s in SEM_NAMES}
        for s in SEM_NAMES:
            self.sems[s] = nc.alloc_semaphore(name="sem_" + s)

    def w(self, e, sem, tgt):
        if tgt is None or tgt <= 0:
            return
        assert tgt <= self.cnt[sem], f"wait on future {sem}: {tgt} > {self.cnt[sem]}"
        h = self.sems[sem]
        self.q[e].append(lambda eng, h=h, tgt=tgt: eng.wait_ge(h, tgt))

    def i(self, e, fn, inc=None, n=1):
        if inc is not None:
            self.cnt[inc] += n
            h = self.sems[inc]
            self.q[e].append(lambda eng, fn=fn, h=h, n=n: fn(eng).then_inc(h, n))
            return self.cnt[inc]
        self.q[e].append(fn)
        return None

    def dma(self, qe, out_ap, in_ap):
        sem = _QUEUE_SEM[qe]
        return self.i(qe, lambda eng, o=out_ap, i_=in_ap: eng.dma_start(o, i_), inc=sem, n=16)

    def ms(self, e):
        return {"t": "te", "v": "ve", "s": "se", "g": "pe"}[e]

    def mm(self, out, lhsT, rhs, start=True, stop=True, tp=None, skip=False, inc=None):
        def f(eng, out=out, lhsT=lhsT, rhs=rhs, start=start, stop=stop, tp=tp, skip=skip):
            kw = {}
            if tp is not None:
                kw["tile_position"] = tp
            if skip:
                kw["skip_group_check"] = True
            return eng.matmul(out, lhsT, rhs, start=start, stop=stop, **kw)
        return self.i("t", f, inc="te" if inc else None)

    def tr(self, out, in_, ident, inc=None):
        return self.i("t", lambda eng, o=out, i_=in_, id_=ident: eng.transpose(o, i_, id_),
                      inc="te" if inc else None)

    def cp(self, e, out, in_, inc=None):
        return self.i(e, lambda eng, o=out, i_=in_: eng.tensor_copy(o, i_),
                      inc=self.ms(e) if inc else None)

    def ts(self, e, out, in0, s1, s2, op0, op1=ALU.bypass, inc=None):
        def f(eng, out=out, in0=in0, s1=s1, s2=s2, op0=op0, op1=op1):
            return eng.tensor_scalar(out, in0, s1, s2, op0, op1)
        return self.i(e, f, inc=self.ms(e) if inc else None)

    def tt(self, e, out, in0, in1, op, inc=None):
        return self.i(e, lambda eng, o=out, a=in0, b=in1, op=op: eng.tensor_tensor(o, a, b, op),
                      inc=self.ms(e) if inc else None)

    def stt(self, e, out, in0, scal, in1, op0, op1, inc=None):
        def f(eng, out=out, in0=in0, scal=scal, in1=in1, op0=op0, op1=op1):
            return eng.scalar_tensor_tensor(out, in0, scal, in1, op0, op1)
        return self.i(e, f, inc=self.ms(e) if inc else None)

    def red(self, e, out, in_, op, absval=False, inc=None):
        def f(eng, out=out, in_=in_, op=op, absval=absval):
            return eng.tensor_reduce(out, in_, AX.X, op, apply_absolute_value=absval)
        return self.i(e, f, inc=self.ms(e) if inc else None)

    def act(self, out, in_, func, bias=0.0, scale=1.0, accum=None, inc=None):
        def f(eng, out=out, in_=in_, func=func, bias=bias, scale=scale, accum=accum):
            kw = {}
            if accum is not None:
                kw["accum_out"] = accum
            return eng.activation(out, in_, func, bias=bias, scale=scale, **kw)
        return self.i("s", f, inc="se" if inc else None)

    def recip(self, out, in_, inc=None):
        return self.i("v", lambda eng, o=out, i_=in_: eng.reciprocal(o, i_),
                      inc="ve" if inc else None)

    def barrier(self):
        snap = [(s, self.cnt[s]) for s in ("te", "ve", "se")]
        own = {"t": "te", "v": "ve", "s": "se"}
        for e in ENG_KEYS:
            for s, tgt in snap:
                if own.get(e) == s:
                    continue
                self.w(e, s, tgt)


class ColAlloc:
    def __init__(self, ncols):
        self.ncols = ncols
        self.off = 0

    def take(self, n):
        assert self.off + n <= self.ncols, f"pool overflow: {self.off}+{n} > {self.ncols}"
        o = self.off
        self.off += n
        return o



def _rope_tables():
    hd = 64
    inv_freq = 1.0 / 10000.0 ** (np.arange(0, hd, 2, dtype=np.float64) / hd)
    pos = np.arange(N, dtype=np.float64)
    ang = pos[:, None] * inv_freq[None, :]
    sin = np.sin(ang).astype(np.float32).T
    cos = np.cos(ang).astype(np.float32).T
    cos128 = np.concatenate([cos, cos, cos, cos], axis=0)
    sin128 = np.concatenate([-sin, sin, -sin, sin], axis=0)
    return cos128.astype(BF), sin128.astype(BF)


def _stage_common(inp):
    st = {}
    cpool = np.zeros((128, _CPN), dtype=np.float32)
    for k, j in CP.items():
        cpool[:, j] = _CP_VALS[k]
    st["cpool"] = cpool
    st["ident_bf"] = np.eye(128, dtype=BF)
    swap = np.zeros((128, 128), dtype=np.float32)
    r = np.arange(128)
    swap[r, (r // 64) * 64 + (r % 64 + 32) % 64] = 1.0
    st["swap_bf"] = swap.astype(BF)
    st["ones_bf"] = np.ones((128, 1), dtype=BF)
    st["onesr_bf"] = np.ones((1, 128), dtype=BF)
    st["zeros_bf"] = np.zeros((128, 512), dtype=BF)
    qi = np.arange(128)[:, None]
    kj = np.arange(128)[None, :]
    st["cmask"] = np.where(kj <= qi, 0.0, -1e9).astype(np.float32)
    cos128, sin128 = _rope_tables()
    st["rope_cos"] = cos128
    st["rope_sin"] = sin128

    g = np.asarray(inp["L0_attn_a_ln_gamma"], np.float32)
    bet = np.asarray(inp["L0_attn_a_ln_beta"], np.float32)
    st["l0aa_gamma_bc"] = np.ascontiguousarray(np.broadcast_to(g, (128, DM)))
    st["l0aa_beta_bc"] = np.ascontiguousarray(np.broadcast_to(bet, (128, DM)))
    st["l0aa_bout_bc"] = np.ascontiguousarray(
        np.broadcast_to(np.asarray(inp["L0_attn_a_b_out"], np.float32), (128, DM)))
    wqkv = np.asarray(inp["L0_attn_a_w_qkv"], dtype=np.float32)
    wq, wk, wv = wqkv[:, :1024], wqkv[:, 1024:2048], wqkv[:, 2048:]
    cols = []
    for pp in range(8):
        hA, hB = 2 * pp, 2 * pp + 1
        for w in (wq, wk):
            blk = np.concatenate([w[:, hA * 64:(hA + 1) * 64],
                                  w[:, hB * 64:(hB + 1) * 64]], axis=1)
            cols.append(blk)
        cols.append(np.concatenate([wv[:, hA * 64:(hA + 1) * 64],
                                    wv[:, hB * 64:(hB + 1) * 64]], axis=1))
    st["l0aa_wqkv"] = np.ascontiguousarray(np.concatenate(cols, axis=1))
    st["l0aa_wout"] = np.asarray(inp["L0_attn_a_w_out"], dtype=np.float32)

    w1 = np.asarray(inp["L0_ffn_w1"], dtype=np.float32)
    w2 = np.asarray(inp["L0_ffn_w2"], dtype=np.float32)
    st["l0ffn_w1T"] = np.ascontiguousarray(w1.T)
    st["l0ffn_w2T"] = np.ascontiguousarray(w2.T)
    st["l0ffn_b1c"] = np.ascontiguousarray(
        np.asarray(inp["L0_ffn_b1"], np.float32).reshape(32, 128).T)
    st["l0ffn_b2bc"] = np.ascontiguousarray(
        np.broadcast_to(np.asarray(inp["L0_ffn_b2"], np.float32), (128, DM)))

    st["l0ab_wthT"] = np.ascontiguousarray(np.asarray(inp["L0_attn_b_W_theta"], np.float32).T)
    st["l0ab_wphT"] = np.ascontiguousarray(np.asarray(inp["L0_attn_b_W_phi"], np.float32).T)
    st["l0ab_wgT"] = np.ascontiguousarray(np.asarray(inp["L0_attn_b_W_g"], np.float32).T)

    st["l1aa_wqT"] = np.ascontiguousarray(np.asarray(inp["L1_attn_a_Wq"], np.float32).T)
    st["l1aa_wkT"] = np.ascontiguousarray(np.asarray(inp["L1_attn_a_Wk"], np.float32).T)
    st["l1aa_wvT"] = np.ascontiguousarray(np.asarray(inp["L1_attn_a_Wv"], np.float32).T)
    st["l1aa_woT"] = np.ascontiguousarray(np.asarray(inp["L1_attn_a_Wo"], np.float32).T)

    st["l1ffn_wg"] = np.asarray(inp["L1_ffn_Wg"], np.float32)
    st["l1ffn_bg_bc"] = np.ascontiguousarray(
        np.broadcast_to(np.asarray(inp["L1_ffn_bg"], np.float32), (128, 8)))
    st["l1ffn_w1"] = np.asarray(inp["L1_ffn_W1"], np.float32)
    st["l1ffn_w2"] = np.asarray(inp["L1_ffn_W2"], np.float32)
    b1 = np.asarray(inp["L1_ffn_b1"], np.float32)
    st["l1ffn_b1c"] = np.ascontiguousarray(
        b1.reshape(8, 16, 128).transpose(2, 0, 1).reshape(128, 128))
    st["l1ffn_b2"] = np.asarray(inp["L1_ffn_b2"], np.float32)

    st["l1ab_wqT"] = np.ascontiguousarray(np.asarray(inp["L1_attn_b_Wq"], np.float32).T)
    st["l1ab_wkT"] = np.ascontiguousarray(np.asarray(inp["L1_attn_b_Wk"], np.float32).T)
    st["l1ab_wvT"] = np.ascontiguousarray(np.asarray(inp["L1_attn_b_Wv"], np.float32).T)
    st["l1ab_woT"] = np.ascontiguousarray(np.asarray(inp["L1_attn_b_Wo"], np.float32).T)
    st["l1ab_lq1T"] = np.ascontiguousarray(np.asarray(inp["L1_attn_b_lq1"], np.float32).T)
    st["l1ab_lk1T"] = np.ascontiguousarray(np.asarray(inp["L1_attn_b_lk1"], np.float32).T)
    st["l1ab_lq2T"] = np.ascontiguousarray(np.asarray(inp["L1_attn_b_lq2"], np.float32).T)
    st["l1ab_lk2T"] = np.ascontiguousarray(np.asarray(inp["L1_attn_b_lk2"], np.float32).T)
    st["l1ab_rms"] = np.ascontiguousarray(
        np.asarray(inp["L1_attn_b_rms_scale"], np.float32).reshape(128, 1))
    return st



POOL_COLS = 30720
POOLF_COLS = 2048


class Bufs:
    pass


def _alloc_bufs(nc, ctx):
    b = Bufs()
    sb = lambda name, shape, dt: ctx.enter_context(nc.sbuf_tensor(name, shape, dt))
    b.x = sb("x_sb", [128, NT * DM], F32)
    b.xnt = sb("xnt_sb", [128, ND * N], BF16)
    b.xn = sb("xn_sb", [128, 2 * DM], BF16)
    b.sqs = sb("sqs_sb", [128, DM], BF16)
    b.st = sb("st_sb", [128, 192], F32)
    b.pool = sb("pool_sb", [128, POOL_COLS], BF16)
    b.poolf = sb("poolf_sb", [128, POOLF_COLS], F32)
    b.cp = sb("cp_sb", [128, _CPN], F32)
    b.ident = sb("ident_sb", [128, 128], BF16)
    b.swap = sb("swap_sb", [128, 128], BF16)
    b.ones = sb("ones_sb", [128, 1], BF16)
    b.onesr = sb("onesr_sb", [1, 128], BF16)
    b.zeros = sb("zeros_sb", [128, 512], BF16)
    b.cmask = sb("cmask_sb", [128, 128], F32)
    b.cos = sb("cos_sb", [128, N], BF16)
    b.sin = sb("sin_sb", [128, N], BF16)
    b.gamma = sb("gamma_sb", [128, DM], F32)
    b.beta = sb("beta_sb", [128, DM], F32)
    b.bout = sb("bout_sb", [128, DM], F32)
    b.smf = sb("smf_sb", [128, 512], F32)
    b.ct = sb("ct_sb", [8, N], BF16)
    b.b2e = sb("b2e_sb", [8, DM], BF16)
    b.smb = sb("smb_sb", [128, 2048], BF16)
    b.psa = ctx.enter_context(nc.psum_tensor("psa", [128, 1024], F32))
    b.psb = ctx.enter_context(nc.psum_tensor("psb", [128, 1024], F32))
    b.psc = ctx.enter_context(nc.psum_tensor("psc", [128, 1024], F32))
    b.pt32 = ctx.enter_context(nc.psum_tensor("pt32", [128, 512], F32))
    b.ptb = ctx.enter_context(nc.psum_tensor("ptb", [128, 1024], BF16))
    return b


def cpc(b, name):
    j = CP[name]
    return b.cp[:, j:j + 1]


def xnt_chunk(b, dc, c0, c1):
    return b.xnt[:, dc * N + c0:dc * N + c1]



def emit_ln_xnt(P, b, x_ready_ve, gamma_beta=False):
    st = b.st
    P.w("v", "ve", x_ready_ve)
    P.w("s", "ve", x_ready_ve)
    ve_red = None
    se_sq = None
    for tt in range(NT):
        xt = b.x[:, tt * DM:(tt + 1) * DM]
        ve_red = P.red("v", st[:, tt:tt + 1], xt, ALU.add, inc=(tt == NT - 1))
        se_sq = P.act(b.sqs[:], xt, AF.Square, accum=st[:, 16 + tt:16 + tt + 1],
                      inc=(tt == NT - 1))
    P.w("v", "se", se_sq)
    mu = st[:, 64:80]; ex2 = st[:, 80:96]; var = st[:, 96:112]; sd = st[:, 112:128]
    vt = P.ts("v", mu, st[:, 0:16], cpc(b, "inv_d"), None, ALU.mult, inc=True)
    vt = P.ts("v", ex2, st[:, 16:32], cpc(b, "inv_d"), None, ALU.mult, inc=True)
    P.w("v", "ve", vt)
    vt = P.tt("v", var, mu, mu, ALU.mult, inc=True)
    P.w("v", "ve", vt)
    ve_t = P.tt("v", var, ex2, var, ALU.subtract, inc=True)
    P.w("s", "ve", ve_t)
    if gamma_beta:
        se_t = P.act(sd, var, AF.Sqrt, bias=cpc(b, "eps_sq"), scale=cpc(b, "one_p_eps"), inc=True)
    else:
        se_t = P.act(sd, var, AF.Sqrt, bias=cpc(b, "eps"), inc=True)
    P.w("v", "se", se_t)
    rstd = st[:, 32:48]; negmur = st[:, 48:64]
    vt = P.recip(rstd, sd, inc=True)
    P.w("v", "ve", vt)
    vt = P.tt("v", negmur, mu, rstd, ALU.mult, inc=True)
    P.w("v", "ve", vt)
    ve_t = P.ts("v", negmur, negmur, -1.0, None, ALU.mult, inc=True)
    P.w("v", "ve", ve_t)

    te_tr = {}
    ve_t = None
    for tt in range(NT):
        xt = b.x[:, tt * DM:(tt + 1) * DM]
        xn = b.xn[:, (tt % 2) * DM:(tt % 2 + 1) * DM]
        if tt - 2 in te_tr:
            P.w("v", "te", te_tr[tt - 2])
        if gamma_beta:
            tmp = b.poolf[:, 0:DM]
            P.ts("v", tmp, xt, st[:, 32 + tt:32 + tt + 1], st[:, 48 + tt:48 + tt + 1],
                 ALU.mult, ALU.add)
            P.stt("v", tmp, tmp, 1.0, b.gamma[:], ALU.mult, ALU.mult)
            ve_t = P.tt("v", xn, tmp, b.beta[:], ALU.add, inc=True)
        else:
            ve_t = P.ts("v", xn, xt, st[:, 32 + tt:32 + tt + 1], st[:, 48 + tt:48 + tt + 1],
                        ALU.mult, ALU.add, inc=True)
        P.w("t", "ve", ve_t)
        te_l = None
        for dc in range(ND):
            te_l = P.tr(b.ptb[:, dc * 128:(dc + 1) * 128], xn[:, dc * 128:(dc + 1) * 128],
                        b.ident[:], inc=(dc == ND - 1))
        te_tr[tt] = te_l
        P.w("v", "te", te_l)
        xnt_view = b.xnt[:].rearrange("p (dc n) -> p dc n", dc=ND)[:, :, tt * 128:(tt + 1) * 128]
        ptb_view = b.ptb[:].rearrange("p (dc j) -> p dc j", dc=ND)
        ve_t = P.cp("v", xnt_view, ptb_view, inc=True)
        P.w("t", "ve", ve_t)
    return ve_t



def emit_l0ffn(P, b, dr, xnt_ve):
    NQ = 4
    HCQ = 8
    TG = 256
    NTG = N // TG

    pool = ColAlloc(POOL_COLS)
    R1 = [pool.take(8192), pool.take(8192)]
    R2 = [pool.take(4096), pool.take(4096)]
    pool = ColAlloc(POOL_COLS)
    R1 = [pool.take(8192), pool.take(8192)]
    R2 = [pool.take(8192)]
    RH = [pool.take(2048), pool.take(2048)]
    RSC = pool.take(1024)
    A2BC = pool.take(1024)

    w1T, w2T = dr["w1T"], dr["w2T"]
    A1 = 0
    b1c = b.smf[:, 64:96]
    t_sm = P.dma("y", b1c, dr["b1c"])
    t_sm = P.dma("y", b.bout[:], dr["b2bc"])
    qs_smalls = t_sm

    te_a2 = None
    qg_prev_read_se = None
    for q in range(NQ):
        reg = R2[0] if q % 2 == 0 else R1[1]
        if qg_prev_read_se is not None:
            P.w("g", "se", qg_prev_read_se)
        qg_t = None
        for hcq in range(HCQ):
            hc = q * HCQ + hcq
            qg_t = P.dma("g", b.pool[:, reg + hcq * 1024: reg + (hcq + 1) * 1024],
                         w2T[hc * 128:(hc + 1) * 128, :])
        P.w("s", "qg", qg_t)
        ve_l = None
        for hcq in range(HCQ):
            hc = q * HCQ + hcq
            wchunk = b.pool[:, reg + hcq * 1024: reg + (hcq + 1) * 1024]
            for cg in range(2):
                scr = b.pool[:, RSC + ((hcq * 2 + cg) % 2) * 512: RSC + ((hcq * 2 + cg) % 2) * 512 + 512]
                if te_a2 is not None:
                    P.w("s", "te", te_a2)
                ve_l = P.act(scr, wchunk[:, cg * 512:(cg + 1) * 512], AF.Abs, inc=True)
                P.w("t", "se", ve_l)
                te_a2 = P.mm(b.psa[0:1, cg * 512:(cg + 1) * 512], b.ones[:], scr,
                             start=(hc == 0), stop=(hc == 4 * HCQ - 1), inc=True)
        qg_prev_read_se = ve_l
    P.w("v", "te", te_a2)
    a2row = b.smb[0:1, 0:1024]
    ve_t = P.ts("v", a2row, b.psa[0:1, 0:1024], 1.0 / 4096.0, None, ALU.mult, inc=True)
    P.w("t", "ve", ve_t)
    te_t = None
    for g2 in range(2):
        te_t = P.mm(b.pt32[:, 0:512], b.onesr[:], a2row[:, g2 * 512:(g2 + 1) * 512],
                    inc=True)
        P.w("v", "te", te_t)
        ve_t = P.cp("v", b.pool[:, A2BC + g2 * 512: A2BC + (g2 + 1) * 512], b.pt32[:, 0:512],
                    inc=True)
        P.w("t", "ve", ve_t)
    ve_a2bc = ve_t

    P.w("s", "qs", qs_smalls)
    prepass_se = qg_prev_read_se
    te_w1_used = [None, None]
    te_w2_used = None
    se_hT_slot = {}
    ve_y_slot = {}
    ve_x = None
    for q in range(NQ):
        p1 = q % 2
        reg1, reg2 = R1[p1], R2[0]
        if te_w1_used[p1] is not None:
            P.w("g", "te", te_w1_used[p1])
        elif p1 == 1:
            P.w("g", "se", prepass_se)
        t1 = None
        for dc in range(ND):
            t1 = P.dma("g", b.pool[:, reg1 + dc * 1024: reg1 + (dc + 1) * 1024],
                       w1T[dc * 128:(dc + 1) * 128, q * 1024:(q + 1) * 1024])
        if te_w2_used is not None:
            P.w("g", "te", te_w2_used)
        elif q == 0:
            P.w("g", "se", prepass_se)
        t2 = None
        for hcq in range(HCQ):
            hc = q * HCQ + hcq
            t2 = P.dma("g", b.pool[:, reg2 + hcq * 1024: reg2 + (hcq + 1) * 1024],
                       w2T[hc * 128:(hc + 1) * 128, :])

        P.w("s", "qg", t1)
        te_a1 = None
        se_abs = None
        for cg in range(2):
            for dc in range(ND):
                wchunk = b.pool[:, reg1 + dc * 1024: reg1 + (dc + 1) * 1024]
                scr = b.pool[:, RSC + (dc % 2) * 512: RSC + (dc % 2) * 512 + 512]
                if te_a1 is not None:
                    P.w("s", "te", te_a1)
                se_abs = P.act(scr, wchunk[:, cg * 512:(cg + 1) * 512], AF.Abs, inc=True)
                P.w("t", "se", se_abs)
                te_a1 = P.mm(b.psb[0:1, cg * 512:(cg + 1) * 512], b.ones[:], scr,
                             start=(dc == 0), stop=(dc == ND - 1), inc=True)
        P.w("v", "te", te_a1)
        a1row = b.smb[0:1, 1024:2048]
        ve_t = P.ts("v", a1row, b.psb[0:1, 0:1024], 1.0 / 1024.0, None, ALU.mult, inc=True)
        P.w("t", "ve", ve_t)
        for hcq in range(HCQ):
            te_t = P.mm(b.pt32[:, 64 + hcq:64 + hcq + 1],
                        a1row[:, hcq * 128:(hcq + 1) * 128], b.ones[0:1, 0:1],
                        inc=(hcq == HCQ - 1))
        P.w("v", "te", te_t)
        ve_a1 = P.cp("v", b.smf[:, A1 + q * 8:A1 + q * 8 + 8], b.pt32[:, 64:72], inc=True)

        se_sign = None
        for dc in range(ND):
            wchunk = b.pool[:, reg1 + dc * 1024: reg1 + (dc + 1) * 1024]
            se_sign = P.act(wchunk, wchunk, AF.Sign, inc=True)
        P.w("s", "qg", t2)
        for hcq in range(HCQ):
            wchunk = b.pool[:, reg2 + hcq * 1024: reg2 + (hcq + 1) * 1024]
            se_sign = P.act(wchunk, wchunk, AF.Sign, inc=True)
        P.w("v", "se", se_sign)
        P.w("v", "ve", ve_a2bc) if q == 0 else None
        ve_bw2 = None
        for hcq in range(HCQ):
            wchunk = b.pool[:, reg2 + hcq * 1024: reg2 + (hcq + 1) * 1024]
            ve_bw2 = P.tt("v", wchunk, wchunk, b.pool[:, A2BC:A2BC + 1024], ALU.mult, inc=True)

        P.w("t", "se", se_sign)
        P.w("t", "ve", ve_bw2)
        if q == 0:
            P.w("t", "ve", xnt_ve)
            P.w("s", "ve", ve_a1)
        else:
            P.w("s", "ve", ve_a1)
        te_last = None
        for tg in range(NTG):
            hreg = RH[tg % 2]
            for hcq in range(HCQ):
                slot = hcq % 2
                ps = b.psc[:, slot * 512: slot * 512 + 256]
                if (slot, "h") in se_hT_slot:
                    P.w("t", "se", se_hT_slot[(slot, "h")])
                for dc in range(ND):
                    lhs = b.pool[:, reg1 + dc * 1024 + hcq * 128: reg1 + dc * 1024 + (hcq + 1) * 128]
                    te_t = P.mm(ps, lhs, xnt_chunk(b, dc, tg * TG, (tg + 1) * TG),
                                start=(dc == 0), stop=(dc == ND - 1), inc=(dc == ND - 1))
                P.w("s", "te", te_t)
                hc_glob = q * HCQ + hcq
                se_t = P.act(b.pool[:, hreg + hcq * 256: hreg + (hcq + 1) * 256], ps, AF.Gelu,
                             bias=b1c[:, hc_glob:hc_glob + 1],
                             scale=b.smf[:, A1 + q * 8 + hcq:A1 + q * 8 + hcq + 1], inc=True)
                se_hT_slot[(slot, "h")] = se_t
            P.w("t", "se", se_t)
            for tsub in range(2):
                tt_tok = tg * 2 + tsub
                for Dg in range(2):
                    slot = (tsub * 2 + Dg) % 2
                    ps = b.psb[:, slot * 512: (slot + 1) * 512]
                    if (slot, "y") in ve_y_slot:
                        P.w("t", "ve", ve_y_slot[(slot, "y")])
                    for hcq in range(HCQ):
                        lhs = b.pool[:, hreg + hcq * 256 + tsub * 128: hreg + hcq * 256 + (tsub + 1) * 128]
                        rhs = b.pool[:, reg2 + hcq * 1024 + Dg * 512: reg2 + hcq * 1024 + (Dg + 1) * 512]
                        te_t = P.mm(ps, lhs, rhs, start=(hcq == 0), stop=(hcq == HCQ - 1),
                                    inc=(hcq == HCQ - 1))
                    te_last = te_t
                    P.w("v", "te", te_t)
                    xsl = b.x[:, tt_tok * DM + Dg * 512: tt_tok * DM + (Dg + 1) * 512]
                    ve_t = P.tt("v", xsl, ps, xsl, ALU.add, inc=True)
                    ve_y_slot[(slot, "y")] = ve_t
        te_w1_used[p1] = te_last
        te_w2_used = te_last
        ve_x = ve_t

    P.w("v", "qs", qs_smalls)
    for tt in range(NT):
        xsl = b.x[:, tt * DM:(tt + 1) * DM]
        ve_x = P.tt("v", xsl, xsl, b.bout[:], ALU.add, inc=True)
    return ve_x




def emit_l0ab(P, b, dr, xnt_ve):
    pool = ColAlloc(POOL_COLS)
    RWT = pool.take(2048)
    RWP = pool.take(2048)
    RWG = pool.take(4096)
    RTH = pool.take(4096)
    RPH = pool.take(4096)
    RG = pool.take(8192)
    RM = pool.take(2048)

    wthT, wphT, wgT = dr["wthT"], dr["wphT"], dr["wgT"]
    t1 = None
    for dc in range(ND):
        t1 = P.dma("g", b.pool[:, RWT + dc * 256: RWT + (dc + 1) * 256],
                   wthT[dc * 128:(dc + 1) * 128, :])
    for dc in range(ND):
        t1 = P.dma("g", b.pool[:, RWP + dc * 256: RWP + (dc + 1) * 256],
                   wphT[dc * 128:(dc + 1) * 128, :])
    P.w("t", "qg", t1)
    P.w("t", "ve", xnt_ve)

    ve_th = None
    for Lc in range(2):
        for tg in range(4):
            ps = b.psc[:, (tg % 2) * 512:(tg % 2) * 512 + 512]
            if ve_th is not None and (Lc * 4 + tg) >= 2:
                P.w("t", "ve", ve_th_slot[(tg % 2)])
            for dc in range(ND):
                lhs = b.pool[:, RWT + dc * 256 + Lc * 128: RWT + dc * 256 + (Lc + 1) * 128]
                te_t = P.mm(ps, lhs, xnt_chunk(b, dc, tg * 512, (tg + 1) * 512),
                            start=(dc == 0), stop=(dc == ND - 1), inc=(dc == ND - 1))
            P.w("v", "te", te_t)
            ve_th = P.cp("v", b.pool[:, RTH + Lc * 2048 + tg * 512: RTH + Lc * 2048 + (tg + 1) * 512],
                         ps, inc=True)
            if Lc * 4 + tg == 0:
                ve_th_slot = {}
            ve_th_slot[(tg % 2)] = ve_th
    ve_ph_slot = {}
    ve_ph = None
    for tt in range(NT):
        ps = b.psc[:, (tt % 2) * 512:(tt % 2) * 512 + 256]
        P.w("t", "ve", ve_th_slot[tt % 2] if tt < 2 else ve_ph_slot[tt % 2])
        for dc in range(ND):
            lhs = xnt_chunk(b, dc, tt * 128, (tt + 1) * 128)
            rhs = b.pool[:, RWP + dc * 256: RWP + (dc + 1) * 256]
            te_t = P.mm(ps, lhs, rhs, start=(dc == 0), stop=(dc == ND - 1), inc=(dc == ND - 1))
        P.w("v", "te", te_t)
        ve_ph = P.cp("v", b.pool[:, RPH + tt * 256: RPH + (tt + 1) * 256], ps, inc=True)
        ve_ph_slot[tt % 2] = ve_ph

    ve_m = {}
    te_g_last = None
    for og in range(2):
        tg_w = None
        if og == 1:
            P.w("g", "te", te_g_last)
        for dc in range(ND):
            tg_w = P.dma("g", b.pool[:, RWG + dc * 512: RWG + (dc + 1) * 512],
                         wgT[dc * 128:(dc + 1) * 128, og * 512:(og + 1) * 512])
        P.w("t", "qg", tg_w)
        ve_g_slot = {}
        for tt in range(NT):
            ps = b.psc[:, (tt % 2) * 512:(tt % 2) * 512 + 512]
            if tt >= 2:
                P.w("t", "ve", ve_g_slot[tt % 2])
            elif og == 0:
                P.w("t", "ve", ve_ph_slot[tt % 2])
            for dc in range(ND):
                lhs = xnt_chunk(b, dc, tt * 128, (tt + 1) * 128)
                rhs = b.pool[:, RWG + dc * 512: RWG + (dc + 1) * 512]
                te_t = P.mm(ps, lhs, rhs, start=(dc == 0), stop=(dc == ND - 1), inc=(dc == ND - 1))
            P.w("v", "te", te_t)
            ve_g = P.cp("v", b.pool[:, RG + tt * 512: RG + (tt + 1) * 512], ps, inc=True)
            ve_g_slot[tt % 2] = ve_g
        P.w("t", "ve", ve_g)
        for Lc in range(2):
            psm = (b.psa if Lc == 0 else b.psb)[:, og * 512:(og + 1) * 512]
            for tt in range(NT):
                lhs = b.pool[:, RPH + tt * 256 + Lc * 128: RPH + tt * 256 + (Lc + 1) * 128]
                rhs = b.pool[:, RG + tt * 512: RG + (tt + 1) * 512]
                te_g_last = P.mm(psm, lhs, rhs, start=(tt == 0), stop=(tt == NT - 1),
                                 inc=(tt == NT - 1))
            ve_m[(Lc, og)] = te_g_last
    ve_t = None
    for Lc in range(2):
        for og in range(2):
            P.w("v", "te", ve_m[(Lc, og)])
            ve_t = P.cp("v", b.pool[:, RM + Lc * 1024 + og * 512: RM + Lc * 1024 + (og + 1) * 512],
                        (b.psa if Lc == 0 else b.psb)[:, og * 512:(og + 1) * 512], inc=True)
    ve_mev = ve_t

    P.w("t", "ve", ve_mev)
    ve_x = None
    ve_o_slot = {}
    for tt in range(NT):
        for og in range(2):
            sl = (tt * 2 + og) % 2
            ps = b.psc[:, sl * 512: sl * 512 + 512]
            if (sl in ve_o_slot):
                P.w("t", "ve", ve_o_slot[sl])
            for Lc in range(2):
                lhs = b.pool[:, RTH + Lc * 2048 + tt * 128: RTH + Lc * 2048 + (tt + 1) * 128]
                rhs = b.pool[:, RM + Lc * 1024 + og * 512: RM + Lc * 1024 + (og + 1) * 512]
                te_t = P.mm(ps, lhs, rhs, start=(Lc == 0), stop=(Lc == 1), inc=(Lc == 1))
            P.w("v", "te", te_t)
            xsl = b.x[:, tt * DM + og * 512: tt * DM + (og + 1) * 512]
            ve_x = P.stt("v", xsl, ps, 1.0 / 32.0, xsl, ALU.mult, ALU.add, inc=True)
            ve_o_slot[sl] = ve_x
    return ve_x




def emit_l1ffn(P, b, dr, xnt_ve):
    pool = ColAlloc(POOL_COLS)
    RW1 = [pool.take(8192), pool.take(8192)]
    RW2 = pool.take(8192)
    RH = [pool.take(2048), pool.take(2048)]
    RWG = pool.take(64)
    RWG2 = pool.take(64)
    RCB = pool.take(8)
    RXL = pool.take(1024)

    CMF = 352
    B1C = 96
    BGB = 224

    t = P.dma("y", b.smf[:, B1C:B1C + 128], dr["mb1c"])
    t = P.dma("y", b.smf[:, BGB:BGB + 8], dr["mbg"])
    wgf = b.poolf[:, 1080:1144]
    for dc in range(ND):
        t = P.dma("y", wgf[:, dc * 8:(dc + 1) * 8], dr["mwg"][dc * 128:(dc + 1) * 128, :])
    qs_sm = t
    tg_w = None
    for dc in range(ND):
        tg_w = P.dma("g", b.pool[:, RWG + dc * 8: RWG + (dc + 1) * 8],
                     dr["mwg"][dc * 128:(dc + 1) * 128, :])
    tg_w = P.dma("g", b.b2e[:], dr["mb2"])
    qg_small = tg_w

    P.w("t", "ve", xnt_ve)
    P.w("t", "qg", qg_small)
    P.w("v", "qs", qs_sm)
    lg = b.poolf[:, 1024:1032]; ex = b.poolf[:, 1032:1040]; pp = b.poolf[:, 1040:1048]
    pm = b.poolf[:, 1048:1056]; cu = b.poolf[:, 1056:1064]; tmp = b.poolf[:, 1064:1072]
    rmax = b.poolf[:, 1072:1073]; nrm = b.poolf[:, 1073:1074]; esum = b.poolf[:, 1074:1075]
    res_ = b.poolf[:, 1075:1076]; m1 = b.poolf[:, 1076:1077]; m2 = b.poolf[:, 1077:1078]
    den = b.poolf[:, 1078:1079]; rden = b.poolf[:, 1079:1080]
    cbf = b.pool[:, RCB:RCB + 8]
    xnf = b.poolf[:, 0:1024]
    P.w("v", "qg", qg_small)
    P.w("v", "qs", qs_sm)
    ve_wl = None
    for dc in range(ND):
        ve_wl = P.tt("v", b.pool[:, RWG2 + dc * 8: RWG2 + (dc + 1) * 8],
                     wgf[:, dc * 8:(dc + 1) * 8], b.pool[:, RWG + dc * 8: RWG + (dc + 1) * 8],
                     ALU.subtract, inc=True)
    P.w("t", "ve", ve_wl)
    ve_c = None
    te_ct_evac_war = None
    te_xl_war = None
    for tt in range(NT):
        vt = P.ts("v", xnf, b.x[:, tt * DM:(tt + 1) * DM], b.st[:, 32 + tt:32 + tt + 1],
                  b.st[:, 48 + tt:48 + tt + 1], ALU.mult, ALU.add, inc=True)
        P.w("v", "ve", vt)
        vt = P.cp("v", b.sqs[:], xnf, inc=True)
        P.w("v", "ve", vt)
        vt = P.tt("v", b.xn[:, 0:DM], xnf, b.sqs[:], ALU.subtract, inc=True)
        P.w("t", "ve", vt)
        if te_xl_war is not None:
            P.w("t", "ve", te_xl_war)
        for dc in range(ND):
            te_t = P.tr(b.ptb[:, dc * 128:(dc + 1) * 128], b.xn[:, dc * 128:(dc + 1) * 128],
                        b.ident[:], inc=(dc == ND - 1))
        P.w("v", "te", te_t)
        xlt_view = b.pool[:, RXL:RXL + 1024].rearrange("p (dc j) -> p dc j", dc=ND)
        ptb_view = b.ptb[:].rearrange("p (dc j) -> p dc j", dc=ND)
        ve_xl = P.cp("v", xlt_view, ptb_view, inc=True)
        P.w("t", "ve", ve_xl)
        ps = b.psc[:, (tt % 2) * 512:(tt % 2) * 512 + 8]
        n_mm = 0
        for dc in range(ND):
            lhs_h = xnt_chunk(b, dc, tt * 128, (tt + 1) * 128)
            lhs_l = b.pool[:, RXL + dc * 128: RXL + (dc + 1) * 128]
            rhs_h = b.pool[:, RWG + dc * 8: RWG + (dc + 1) * 8]
            rhs_l = b.pool[:, RWG2 + dc * 8: RWG2 + (dc + 1) * 8]
            for (lh, rh) in ((lhs_h, rhs_h), (lhs_h, rhs_l), (lhs_l, rhs_h)):
                te_t = P.mm(ps, lh, rh, start=(n_mm == 0), stop=(n_mm == 23),
                            inc=(n_mm == 23))
                n_mm += 1
        te_xl_war = te_t
        P.w("v", "te", te_t)

        def ve_step(fn):
            tgt = fn()
            P.w("v", "ve", tgt)
            return tgt

        ve_step(lambda: P.tt("v", lg, ps, b.smf[:, BGB:BGB + 8], ALU.add, inc=True))
        ve_step(lambda: P.red("v", rmax, lg, ALU.max, inc=True))
        nrm_t = ve_step(lambda: P.ts("v", nrm, rmax, -1.0, None, ALU.mult, inc=True))
        P.w("s", "ve", nrm_t)
        se_t = P.act(ex, lg, AF.Exp, bias=nrm, accum=esum, inc=True)
        P.w("v", "se", se_t)
        ve_step(lambda: P.recip(res_, esum, inc=True))
        ve_step(lambda: P.ts("v", pp, ex, res_, None, ALU.mult, inc=True))
        ve_step(lambda: P.red("v", m1, pp, ALU.max, inc=True))
        ve_step(lambda: P.ts("v", tmp, pp, m1, None, ALU.is_equal, inc=True))
        ve_step(lambda: P.tt("v", tmp, pp, tmp, ALU.mult, inc=True))
        ve_step(lambda: P.tt("v", pm, pp, tmp, ALU.subtract, inc=True))
        ve_step(lambda: P.red("v", m2, pm, ALU.max, inc=True))
        ve_step(lambda: P.ts("v", tmp, pp, m2, None, ALU.is_ge, inc=True))
        ve_step(lambda: P.tt("v", cu, pp, tmp, ALU.mult, inc=True))
        ve_step(lambda: P.red("v", den, cu, ALU.add, inc=True))
        ve_step(lambda: P.recip(rden, den, inc=True))
        ve_c = ve_step(lambda: P.ts("v", b.smf[:, CMF + tt * 8:CMF + (tt + 1) * 8],
                                    cu, rden, None, ALU.mult, inc=True))
        ve_cb = ve_step(lambda: P.cp("v", cbf, b.smf[:, CMF + tt * 8:CMF + (tt + 1) * 8],
                                     inc=True))
        P.w("t", "ve", ve_cb)
        if te_ct_evac_war is not None and tt % 8 == 0 and tt > 0:
            P.w("t", "ve", te_ct_evac_war)
        te_tr = P.tr(b.ptb[0:8, (tt % 8) * 128:(tt % 8 + 1) * 128], cbf, b.ident[:], inc=True)
        P.w("v", "te", te_tr)
        te_ct_evac_war = P.cp("v", b.ct[:, tt * 128:(tt + 1) * 128],
                              b.ptb[0:8, (tt % 8) * 128:(tt % 8 + 1) * 128], inc=True)
    ve_ct = te_ct_evac_war
    ve_gate = ve_c

    P.w("t", "ve", ve_ct)
    ve_b2_slot = {}
    for tt in range(NT):
        for Dg in range(2):
            sl = (tt * 2 + Dg) % 2
            ps = b.psa[:, sl * 512:(sl + 1) * 512]
            if sl in ve_b2_slot:
                P.w("t", "ve", ve_b2_slot[sl])
            te_t = P.mm(ps, b.ct[:, tt * 128:(tt + 1) * 128], b.b2e[:, Dg * 512:(Dg + 1) * 512],
                        inc=True)
            P.w("v", "te", te_t)
            xsl = b.x[:, tt * DM + Dg * 512: tt * DM + (Dg + 1) * 512]
            ve_b2_slot[sl] = P.tt("v", xsl, ps, xsl, ALU.add, inc=True)
    ve_x = ve_b2_slot[sl]

    te_w1_war = [None, None]
    te_w2_war = None
    se_h_slot = {}
    ve_y_slot = {}
    for e in range(8):
        for hf in range(2):
            par = (e * 2 + hf) % 2
            reg1 = RW1[par]
            if te_w1_war[par] is not None:
                P.w("g", "te", te_w1_war[par])
            t1 = None
            for dc in range(ND):
                t1 = P.dma("g", b.pool[:, reg1 + dc * 1024: reg1 + (dc + 1) * 1024],
                           dr["mw1"][e][dc * 128:(dc + 1) * 128, hf * 1024:(hf + 1) * 1024])
            if te_w2_war is not None:
                P.w("g", "te", te_w2_war)
            t2 = None
            for hc in range(8):
                t2 = P.dma("g", b.pool[:, RW2 + hc * 1024: RW2 + (hc + 1) * 1024],
                           dr["mw2"][e][(hf * 8 + hc) * 128:(hf * 8 + hc + 1) * 128, :])
            P.w("t", "qg", t1)
            te_h_last = None
            te_y_last = None
            for tg in range(8):
                hreg = RH[tg % 2]
                for hc in range(8):
                    sl = hc % 2
                    ps = b.psc[:, sl * 512: sl * 512 + 256]
                    if (sl, "h") in se_h_slot:
                        P.w("t", "se", se_h_slot[(sl, "h")])
                    for dc in range(ND):
                        lhs = b.pool[:, reg1 + dc * 1024 + hc * 128: reg1 + dc * 1024 + (hc + 1) * 128]
                        te_t = P.mm(ps, lhs, xnt_chunk(b, dc, tg * 256, (tg + 1) * 256),
                                    start=(dc == 0), stop=(dc == ND - 1), inc=(dc == ND - 1))
                    te_h_last = te_t
                    P.w("s", "te", te_t)
                    bcol = B1C + e * 16 + hf * 8 + hc
                    se_t = P.act(b.pool[:, hreg + hc * 256: hreg + (hc + 1) * 256], ps, AF.Relu,
                                 bias=b.smf[:, bcol:bcol + 1], inc=True)
                    se_h_slot[(sl, "h")] = se_t
                P.w("t", "se", se_t)
                P.w("t", "qg", t2) if tg == 0 else None
                for tsub in range(2):
                    tt_tok = tg * 2 + tsub
                    for Dg in range(2):
                        sl = (tsub * 2 + Dg) % 2
                        ps = b.psb[:, sl * 512:(sl + 1) * 512]
                        if (sl, "y") in ve_y_slot:
                            P.w("t", "ve", ve_y_slot[(sl, "y")])
                        for hc in range(8):
                            lhs = b.pool[:, hreg + hc * 256 + tsub * 128: hreg + hc * 256 + (tsub + 1) * 128]
                            rhs = b.pool[:, RW2 + hc * 1024 + Dg * 512: RW2 + hc * 1024 + (Dg + 1) * 512]
                            te_t = P.mm(ps, lhs, rhs, start=(hc == 0), stop=(hc == 7),
                                        inc=(hc == 7))
                        te_y_last = te_t
                        P.w("v", "te", te_t)
                        xsl = b.x[:, tt_tok * DM + Dg * 512: tt_tok * DM + (Dg + 1) * 512]
                        ccol = CMF + tt_tok * 8 + e
                        ve_t = P.stt("v", xsl, ps, b.smf[:, ccol:ccol + 1], xsl,
                                     ALU.mult, ALU.add, inc=True)
                        ve_y_slot[(sl, "y")] = ve_t
            te_w1_war[par] = te_h_last
            te_w2_war = te_y_last
            ve_x = ve_t
    return ve_x




ATTN_PAIRS = 8
ATTN_QTPS = 8

def emit_attn_a(P, b, dr, xnt_ve, causal, rope, wq_slice, wk_slice, wv_slice, wo_slice,
                bout=False):
    pool = ColAlloc(POOL_COLS)
    RQ = pool.take(2048)
    RK = pool.take(2048)
    RV = pool.take(2048)
    RABS = pool.take(2048)
    RP = pool.take(8192)
    RPT = pool.take(8192)
    RDG = pool.take(256)
    ROT = pool.take(512)
    RW = pool.take(3072)
    RWO = pool.take(1024)
    RMK = pool.take(16)

    NB = 241; RS = 243; RSUM = 247; IVS = 249

    ve_x = None
    te_w_war = None
    war_sc = [None, None]
    war_pv = None
    war_y = {0: None, 1: None}
    war_pt32 = None
    war_rot = [None, None]
    war_p = {}
    war_dg = [None, None]

    def bw(e, war):
        if war is not None:
            P.w(e, war[0], war[1])

    for pp in range(ATTN_PAIRS):
        if te_w_war is not None:
            P.w("g", "te", te_w_war)
        for dc in range(ND):
            P.dma("g", b.pool[:, RW + dc * 384: RW + dc * 384 + 128], wq_slice(dr, dc, pp))
            P.dma("g", b.pool[:, RW + dc * 384 + 128: RW + dc * 384 + 256], wk_slice(dr, dc, pp))
            t1 = P.dma("g", b.pool[:, RW + dc * 384 + 256: RW + dc * 384 + 384], wv_slice(dr, dc, pp))
        t2 = P.dma("g", b.pool[:, RWO:RWO + 1024], wo_slice(dr, pp))
        P.w("t", "qg", t1)
        if pp == 0:
            P.w("t", "ve", xnt_ve)

        ve_qk = None
        for (reg, woff) in ((RQ, 0), (RK, 128)):
            for tg in range(4):
                bw("t", war_sc[0])
                ps = b.psa[:, (tg % 2) * 512:(tg % 2) * 512 + 512]
                for dc in range(ND):
                    lhs = b.pool[:, RW + dc * 384 + woff: RW + dc * 384 + woff + 128]
                    te_t = P.mm(ps, lhs, xnt_chunk(b, dc, tg * 512, (tg + 1) * 512),
                                start=(dc == 0), stop=(dc == ND - 1), inc=(dc == ND - 1))
                P.w("v", "te", te_t)
                ve_qk = P.cp("v", b.pool[:, reg + tg * 512: reg + (tg + 1) * 512], ps, inc=True)
                war_sc[0] = ("ve", ve_qk)

        ve_v = None
        for tt in range(NT):
            bw("t", war_sc[1])
            ps = b.psb[:, (tt % 2) * 512:(tt % 2) * 512 + 128]
            for dc in range(ND):
                lhs = xnt_chunk(b, dc, tt * 128, (tt + 1) * 128)
                rhs = b.pool[:, RW + dc * 384 + 256: RW + dc * 384 + 384]
                te_t = P.mm(ps, lhs, rhs, start=(dc == 0), stop=(dc == ND - 1),
                            inc=(dc == ND - 1))
            P.w("v", "te", te_t)
            ve_v = P.cp("v", b.pool[:, RV + tt * 128: RV + (tt + 1) * 128], ps, inc=True)
            war_sc[1] = ("ve", ve_v)

        if rope:
            ve_rot = ve_v
            for reg in (RQ, RK):
                for half in range(2):
                    c0 = half * 1024
                    bw("t", war_sc[0])
                    P.w("t", "ve", ve_rot)
                    ps = b.psa[:, 0:1024]
                    for j in range(2):
                        te_t = P.mm(ps[:, j * 512:(j + 1) * 512], b.swap[:],
                                    b.pool[:, reg + c0 + j * 512: reg + c0 + (j + 1) * 512],
                                    inc=(j == 1))
                    P.w("v", "te", te_t)
                    tmp = b.poolf[:, 0:1024]
                    t2f = b.poolf[:, 1024:2048]
                    vt = P.tt("v", tmp, ps, b.sin[:, c0:c0 + 1024], ALU.mult, inc=True)
                    war_sc[0] = ("ve", vt)
                    vt2 = P.tt("v", t2f, b.pool[:, reg + c0:reg + c0 + 1024],
                               b.cos[:, c0:c0 + 1024], ALU.mult, inc=True)
                    P.w("v", "ve", vt2)
                    ve_rot = P.tt("v", b.pool[:, reg + c0:reg + c0 + 1024], tmp, t2f,
                                  ALU.add, inc=True)
            ve_qk_done = ve_rot
        else:
            ve_qk_done = ve_qk

        P.w("v", "ve", ve_qk_done)
        ve_mk = P.red("v", b.pool[:, RMK:RMK + 1], b.pool[:, RK:RK + 2048], ALU.max,
                      absval=True, inc=True)
        P.w("v", "ve", ve_mk)
        P.w("s", "ve", ve_qk_done)
        se_absq = P.act(b.pool[:, RABS:RABS + 2048], b.pool[:, RQ:RQ + 2048], AF.Abs, inc=True)
        P.w("t", "se", se_absq)
        P.w("t", "ve", ve_mk)
        war_sc[0] = war_sc[0] or ("ve", ve_qk_done)
        war_sc[1] = war_sc[1] or ("ve", ve_qk_done)

        sr_i = 0
        for qtp in range(ATTN_QTPS):
            qts = [2 * qtp, 2 * qtp + 1]
            ve_pt = None
            for par, qt in enumerate(qts):
                ext = (qt + 1) * 128 if causal else N
                hsubs = (0, 64)
                bw("t", war_pt32)
                bw("t", war_y[1])
                for hi, hb in enumerate(hsubs):
                    te_t = P.mm(b.pt32[:, hi:hi + 1],
                                b.pool[hb:hb + 64, RABS + qt * 128: RABS + (qt + 1) * 128],
                                b.pool[hb:hb + 64, RMK:RMK + 1], inc=(hi == 1))
                P.w("v", "te", te_t)
                vt = P.ts("v", b.smf[:, NB:NB + 2], b.pt32[:, 0:2], -0.125, None,
                          ALU.mult, inc=True)
                P.w("v", "ve", vt)
                P.w("s", "ve", vt)
                war_pt32 = ("ve", vt)
                for hi, hb in enumerate(hsubs):
                    preg = RP + hi * 4096 + par * 2048
                    if (hi, par) in war_p:
                        bw("s", war_p[(hi, par)])
                    nkh = (ext + 1023) // 1024
                    accs = []
                    for kh in range(nkh):
                        w = min(1024, ext - kh * 1024)
                        ri = sr_i % 2; sr_i += 1
                        reg = (b.psa, b.psb)[ri]
                        bw("t", war_sc[ri])
                        nkg = (w + 511) // 512
                        for kg in range(nkg):
                            w2 = min(512, w - kg * 512)
                            te_t = P.mm(reg[:, kg * 512:kg * 512 + w2],
                                        b.pool[hb:hb + 64, RQ + qt * 128: RQ + (qt + 1) * 128],
                                        b.pool[hb:hb + 64, RK + kh * 1024 + kg * 512:
                                               RK + kh * 1024 + kg * 512 + w2],
                                        inc=(kg == nkg - 1))
                        if causal and kh == nkh - 1:
                            P.w("v", "te", te_t)
                            off = qt * 128 - kh * 1024
                            vt = P.tt("v", reg[:, off:off + 128], reg[:, off:off + 128],
                                      b.cmask[:], ALU.add, inc=True)
                            P.w("s", "ve", vt)
                        else:
                            P.w("s", "te", te_t)
                        acc = b.smf[:, RS + kh:RS + kh + 1]
                        se_t = P.act(b.pool[:, preg + kh * 1024: preg + kh * 1024 + w],
                                     reg[:, 0:w], AF.Exp, scale=cpc(b, "scale_attn"),
                                     bias=b.smf[:, NB + hi:NB + hi + 1], accum=acc, inc=True)
                        war_sc[ri] = ("se", se_t)
                        accs.append(acc)
                    P.w("v", "se", se_t)
                    if len(accs) == 2:
                        vt = P.tt("v", b.smf[:, RSUM + hi:RSUM + hi + 1], accs[0], accs[1],
                                  ALU.add, inc=True)
                    else:
                        vt = P.cp("v", b.smf[:, RSUM + hi:RSUM + hi + 1], accs[0], inc=True)
                    P.w("v", "ve", vt)
                    vt = P.recip(b.smf[:, IVS + hi:IVS + hi + 1],
                                 b.smf[:, RSUM + hi:RSUM + hi + 1], inc=True)
                    P.w("v", "ve", vt)
                    bw("v", war_dg[hi])
                    ve_diag = P.ts("v", b.pool[:, RDG + hi * 128: RDG + (hi + 1) * 128],
                                   b.ident[:], b.smf[:, IVS + hi:IVS + hi + 1], None,
                                   ALU.mult, inc=True)
                    P.w("t", "ve", ve_diag)
                    nkc = ext // 128
                    for kc0 in range(0, nkc, 4):
                        kcs = list(range(kc0, min(kc0 + 4, nkc)))
                        bw("t", war_pt32)
                        for kc in kcs:
                            te_t = P.mm(b.pt32[:, (kc % 4) * 128:(kc % 4 + 1) * 128],
                                        b.pool[:, preg + kc * 128: preg + (kc + 1) * 128],
                                        b.pool[:, RDG + hi * 128: RDG + (hi + 1) * 128],
                                        inc=(kc == kcs[-1]))
                        P.w("v", "te", te_t)
                        nk = len(kcs)
                        base = RPT + hi * 4096 + kc0 * 256 + par * 128
                        dstv = b.pool[:, base: base + nk * 256].rearrange(
                            "p (kc j) -> p kc j", kc=nk)[:, :, 0:128]
                        srcv = b.pt32[:, 0:nk * 128].rearrange("p (kc j) -> p kc j", kc=nk)
                        ve_pt = P.cp("v", dstv, srcv, inc=True)
                        war_pt32 = ("ve", ve_pt)
                    war_p[(hi, par)] = ("te", te_t)
                    war_dg[hi] = ("te", te_t)
            if causal:
                ext0 = (qts[0] + 1) * 128
                ext1 = (qts[1] + 1) * 128
                vt = None
                for hi in range(2):
                    for kc in range(ext0 // 128, ext1 // 128):
                        base = RPT + hi * 4096 + kc * 256
                        vt = P.i("v", lambda eng, ap=b.pool[:, base:base + 128]:
                                 eng.memset(ap, 0), inc="ve")
                if vt is not None:
                    P.w("t", "ve", vt)
            P.w("t", "ve", ve_pt)

            nkc = ((qts[1] + 1) * 128 if causal else N) // 128
            pvps = b.psc[:, 0:256]
            bw("t", war_pv)
            bw("t", war_y[0])
            for hi in range(2):
                outap = pvps[0:64, :] if hi == 0 else pvps[64:128, :]
                for kc in range(nkc):
                    te_t = P.mm(outap,
                                b.pool[:, RV + kc * 128 + hi * 64: RV + kc * 128 + hi * 64 + 64],
                                b.pool[:, RPT + hi * 4096 + kc * 256: RPT + hi * 4096 + (kc + 1) * 256],
                                start=(kc == 0), stop=(kc == nkc - 1),
                                tp=None if hi == 0 else (0, 64), skip=(hi == 1),
                                inc=(kc == nkc - 1))
            P.w("v", "te", te_t)
            rotb = b.pool[:, ROT + (qtp % 2) * 256: ROT + (qtp % 2) * 256 + 256]
            bw("v", war_rot[qtp % 2])
            ve_ot = P.cp("v", rotb, pvps, inc=True)
            war_pv = ("ve", ve_ot)
            P.w("t", "ve", ve_ot)

            te_y_last = None
            for tsub in range(2):
                tt_tok = qtp * 2 + tsub
                for Dg in range(2):
                    sl = (tsub * 2 + Dg) % 2
                    ps = b.psc[:, 512:1024] if sl == 0 else b.pt32[:, 0:512]
                    bw("t", war_y[sl])
                    te_t = P.mm(ps, rotb[:, tsub * 128:(tsub + 1) * 128],
                                b.pool[:, RWO + Dg * 512: RWO + (Dg + 1) * 512], inc=True)
                    te_y_last = te_t
                    P.w("v", "te", te_t)
                    xsl = b.x[:, tt_tok * DM + Dg * 512: tt_tok * DM + (Dg + 1) * 512]
                    ve_t = P.tt("v", xsl, ps, xsl, ALU.add, inc=True)
                    war_y[sl] = ("ve", ve_t)
            war_rot[qtp % 2] = ("te", te_y_last)
            te_w_war = te_y_last
            ve_x = ve_t
    if bout:
        P.w("v", "qs", P.cnt["qs"])
        for tt in range(NT):
            xsl = b.x[:, tt * DM:(tt + 1) * DM]
            ve_x = P.tt("v", xsl, xsl, b.bout[:], ALU.add, inc=True)
    return ve_x




def emit_l1ab(P, b, dr, xnt_ve):
    pool = ColAlloc(POOL_COLS)
    RQ = pool.take(2048)
    RK = pool.take(2048)
    RV = pool.take(2048)
    RABS = pool.take(2048)
    RP = pool.take(8192)
    RPT = pool.take(4096)
    RDG = pool.take(256)
    ROT = pool.take(512)
    RW = pool.take(3072)
    RWO = pool.take(1024)
    RMK = pool.take(16)
    RLP = pool.take(64)

    NB = 241; RS = 243; RSUM = 247; IVS = 249
    LAMBC = 208
    RMS02 = 224
    IVR = 226
    SSQ = 228
    LQ = 256

    t = P.dma("y", b.smf[0:64, LQ:LQ + 16], dr["dlq1T"])
    t = P.dma("y", b.smf[0:64, LQ + 16:LQ + 32], dr["dlk1T"])
    t = P.dma("y", b.smf[0:64, LQ + 32:LQ + 48], dr["dlq2T"])
    t = P.dma("y", b.smf[0:64, LQ + 48:LQ + 64], dr["dlk2T"])
    t = P.dma("y", b.smf[:, RMS02:RMS02 + 1], dr["drms"])
    qs_sm = t

    P.w("v", "qs", qs_sm)
    vt = P.ts("v", b.smf[:, RMS02:RMS02 + 1], b.smf[:, RMS02:RMS02 + 1], 0.2, None,
              ALU.mult, inc=True)
    P.w("v", "ve", vt)
    ve_rms02 = vt
    lamrow = b.poolf[0:1, 0:16]
    e2row = b.poolf[0:1, 16:32]
    for i in range(2):
        vt = P.tt("v", b.pool[0:64, RLP:RLP + 16], b.smf[0:64, LQ + i * 32:LQ + i * 32 + 16],
                  b.smf[0:64, LQ + i * 32 + 16:LQ + i * 32 + 32], ALU.mult, inc=True)
        P.w("t", "ve", vt)
        te_t = P.mm(b.pt32[0:1, 0:16], b.ones[0:64, :], b.pool[0:64, RLP:RLP + 16], inc=True)
        P.w("s", "te", te_t)
        se_t = P.act(lamrow if i == 0 else e2row, b.pt32[0:1, 0:16], AF.Exp, inc=True)
        P.w("v", "se", se_t)
        P.w("t", "se", se_t)
    vt = P.tt("v", lamrow, lamrow, e2row, ALU.subtract, inc=True)
    P.w("v", "ve", vt)
    vt = P.ts("v", lamrow, lamrow, 0.8, None, ALU.add, inc=True)
    P.w("v", "ve", vt)
    lamb = b.pool[0:1, RLP + 16:RLP + 32]
    vt = P.cp("v", lamb, lamrow, inc=True)
    P.w("t", "ve", vt)
    for h in range(16):
        te_t = P.mm(b.pt32[:, 64 + h:64 + h + 1], b.onesr[:], lamb[:, h:h + 1],
                    inc=(h == 15))
    P.w("v", "te", te_t)
    ve_lam = P.cp("v", b.smf[:, LAMBC:LAMBC + 16], b.pt32[:, 64:80], inc=True)

    ve_x = None
    te_w_war = None
    war_sc = [None, None]
    war_pv = None
    war_y = {0: None, 1: None}
    war_pt32 = ("ve", ve_lam)
    war_rot = [None, None]
    war_p = {}
    war_dg = [None, None]
    war_ptb = None

    def bw(e, war):
        if war is not None:
            P.w(e, war[0], war[1])

    for h in range(16):
        if te_w_war is not None:
            P.w("g", "te", te_w_war)
        for dc in range(ND):
            P.dma("g", b.pool[:, RW + dc * 384: RW + dc * 384 + 128],
                  dr["bwqT"][dc * 128:(dc + 1) * 128, h * 128:(h + 1) * 128])
            P.dma("g", b.pool[:, RW + dc * 384 + 128: RW + dc * 384 + 256],
                  dr["bwkT"][dc * 128:(dc + 1) * 128, h * 128:(h + 1) * 128])
            t1 = P.dma("g", b.pool[:, RW + dc * 384 + 256: RW + dc * 384 + 384],
                       dr["bwvT"][dc * 128:(dc + 1) * 128, h * 128:(h + 1) * 128])
        t2 = P.dma("g", b.pool[:, RWO:RWO + 1024], dr["bwoT"][h * 128:(h + 1) * 128, :])
        P.w("v", "qg", t2)
        P.w("v", "ve", ve_rms02) if h == 0 else None
        ve_wo = P.ts("v", b.pool[:, RWO:RWO + 1024], b.pool[:, RWO:RWO + 1024],
                     b.smf[:, RMS02:RMS02 + 1], None, ALU.mult, inc=True)
        P.w("t", "qg", t1)
        if h == 0:
            P.w("t", "ve", xnt_ve)

        ve_qk = None
        for (reg, woff) in ((RQ, 0), (RK, 128)):
            for tg in range(4):
                bw("t", war_sc[0])
                ps = b.psa[:, (tg % 2) * 512:(tg % 2) * 512 + 512]
                for dc in range(ND):
                    lhs = b.pool[:, RW + dc * 384 + woff: RW + dc * 384 + woff + 128]
                    te_t = P.mm(ps, lhs, xnt_chunk(b, dc, tg * 512, (tg + 1) * 512),
                                start=(dc == 0), stop=(dc == ND - 1), inc=(dc == ND - 1))
                P.w("v", "te", te_t)
                ve_qk = P.cp("v", b.pool[:, reg + tg * 512: reg + (tg + 1) * 512], ps, inc=True)
                war_sc[0] = ("ve", ve_qk)

        ve_v = None
        for tt in range(NT):
            bw("t", war_sc[1])
            ps = b.psb[:, (tt % 2) * 512:(tt % 2) * 512 + 128]
            for dc in range(ND):
                lhs = xnt_chunk(b, dc, tt * 128, (tt + 1) * 128)
                rhs = b.pool[:, RW + dc * 384 + 256: RW + dc * 384 + 384]
                te_t = P.mm(ps, lhs, rhs, start=(dc == 0), stop=(dc == ND - 1),
                            inc=(dc == ND - 1))
            P.w("v", "te", te_t)
            ve_v = P.cp("v", b.pool[:, RV + tt * 128: RV + (tt + 1) * 128], ps, inc=True)
            war_sc[1] = ("ve", ve_v)

        P.w("v", "ve", ve_v)
        ve_mk = P.red("v", b.pool[:, RMK:RMK + 1], b.pool[:, RK:RK + 2048], ALU.max,
                      absval=True, inc=True)
        P.w("v", "ve", ve_mk)
        P.w("s", "ve", ve_v)
        se_absq = P.act(b.pool[:, RABS:RABS + 2048], b.pool[:, RQ:RQ + 2048], AF.Abs, inc=True)
        P.w("t", "se", se_absq)
        P.w("t", "ve", ve_mk)
        war_sc[0] = war_sc[0] or ("ve", ve_mk)
        war_sc[1] = war_sc[1] or ("ve", ve_mk)

        sr_i = 0
        for qtp in range(8):
            qts = [2 * qtp, 2 * qtp + 1]
            ve_pt = None
            for par, qt in enumerate(qts):
                ext = (qt + 1) * 128
                hsubs = (0, 64)
                bw("t", war_pt32)
                bw("t", war_y[1])
                for hi, hb in enumerate(hsubs):
                    te_t = P.mm(b.pt32[:, hi:hi + 1],
                                b.pool[hb:hb + 64, RABS + qt * 128: RABS + (qt + 1) * 128],
                                b.pool[hb:hb + 64, RMK:RMK + 1], inc=(hi == 1))
                P.w("v", "te", te_t)
                vt = P.ts("v", b.smf[:, NB:NB + 2], b.pt32[:, 0:2], -0.125, None,
                          ALU.mult, inc=True)
                P.w("v", "ve", vt)
                P.w("s", "ve", vt)
                war_pt32 = ("ve", vt)
                for hi, hb in enumerate(hsubs):
                    preg = RP + hi * 4096 + par * 2048
                    if (hi, par) in war_p:
                        bw("s", war_p[(hi, par)])
                    nkh = (ext + 1023) // 1024
                    accs = []
                    for kh in range(nkh):
                        w = min(1024, ext - kh * 1024)
                        ri = sr_i % 2; sr_i += 1
                        reg = (b.psa, b.psb)[ri]
                        bw("t", war_sc[ri])
                        nkg = (w + 511) // 512
                        for kg in range(nkg):
                            w2 = min(512, w - kg * 512)
                            te_t = P.mm(reg[:, kg * 512:kg * 512 + w2],
                                        b.pool[hb:hb + 64, RQ + qt * 128: RQ + (qt + 1) * 128],
                                        b.pool[hb:hb + 64, RK + kh * 1024 + kg * 512:
                                               RK + kh * 1024 + kg * 512 + w2],
                                        inc=(kg == nkg - 1))
                        if kh == nkh - 1:
                            P.w("v", "te", te_t)
                            off = qt * 128 - kh * 1024
                            vt = P.tt("v", reg[:, off:off + 128], reg[:, off:off + 128],
                                      b.cmask[:], ALU.add, inc=True)
                            P.w("s", "ve", vt)
                        else:
                            P.w("s", "te", te_t)
                        acc = b.smf[:, RS + 2 * hi + kh:RS + 2 * hi + kh + 1]
                        se_t = P.act(b.pool[:, preg + kh * 1024: preg + kh * 1024 + w],
                                     reg[:, 0:w], AF.Exp, scale=cpc(b, "scale_attn"),
                                     bias=b.smf[:, NB + hi:NB + hi + 1], accum=acc, inc=True)
                        war_sc[ri] = ("se", se_t)
                        accs.append(acc)
                    P.w("v", "se", se_t)
                    if len(accs) == 2:
                        vt = P.tt("v", b.smf[:, RSUM + hi:RSUM + hi + 1], accs[0], accs[1],
                                  ALU.add, inc=True)
                    else:
                        vt = P.cp("v", b.smf[:, RSUM + hi:RSUM + hi + 1], accs[0], inc=True)
                    P.w("v", "ve", vt)
                    vt = P.recip(b.smf[:, IVS + hi:IVS + hi + 1],
                                 b.smf[:, RSUM + hi:RSUM + hi + 1], inc=True)
                    P.w("v", "ve", vt)
                    if hi == 1:
                        vt = P.ts("v", b.smf[:, IVS + 1:IVS + 2], b.smf[:, IVS + 1:IVS + 2],
                                  b.smf[:, LAMBC + h:LAMBC + h + 1], -1.0, ALU.mult, ALU.mult,
                                  inc=True)
                        P.w("v", "ve", vt)
                    bw("v", war_dg[hi])
                    ve_diag = P.ts("v", b.pool[:, RDG + hi * 128: RDG + (hi + 1) * 128],
                                   b.ident[:], b.smf[:, IVS + hi:IVS + hi + 1], None,
                                   ALU.mult, inc=True)
                    P.w("t", "ve", ve_diag)
                nkc = ext // 128
                for kc0 in range(0, nkc, 4):
                    kcs = list(range(kc0, min(kc0 + 4, nkc)))
                    bw("t", war_pt32)
                    for kc in kcs:
                        for hi in range(2):
                            preg = RP + hi * 4096 + par * 2048
                            te_t = P.mm(b.pt32[:, (kc % 4) * 128:(kc % 4 + 1) * 128],
                                        b.pool[:, preg + kc * 128: preg + (kc + 1) * 128],
                                        b.pool[:, RDG + hi * 128: RDG + (hi + 1) * 128],
                                        start=(hi == 0), stop=(hi == 1),
                                        inc=(kc == kcs[-1] and hi == 1))
                    P.w("v", "te", te_t)
                    nk = len(kcs)
                    base = RPT + kc0 * 256 + par * 128
                    dstv = b.pool[:, base: base + nk * 256].rearrange(
                        "p (kc j) -> p kc j", kc=nk)[:, :, 0:128]
                    srcv = b.pt32[:, 0:nk * 128].rearrange("p (kc j) -> p kc j", kc=nk)
                    ve_pt = P.cp("v", dstv, srcv, inc=True)
                    war_pt32 = ("ve", ve_pt)
                war_p[(0, par)] = ("te", te_t)
                war_p[(1, par)] = ("te", te_t)
                war_dg[0] = ("te", te_t)
                war_dg[1] = ("te", te_t)
            ext0 = (qts[0] + 1) * 128
            ext1 = (qts[1] + 1) * 128
            vt = None
            for kc in range(ext0 // 128, ext1 // 128):
                base = RPT + kc * 256
                vt = P.i("v", lambda eng, ap=b.pool[:, base:base + 128]:
                         eng.memset(ap, 0), inc="ve")
            if vt is not None:
                P.w("t", "ve", vt)
            P.w("t", "ve", ve_pt)

            nkc = ext1 // 128
            pvps = b.psc[:, 0:256]
            bw("t", war_pv)
            for kc in range(nkc):
                te_t = P.mm(pvps, b.pool[:, RV + kc * 128: RV + (kc + 1) * 128],
                            b.pool[:, RPT + kc * 256: RPT + (kc + 1) * 256],
                            start=(kc == 0), stop=(kc == nkc - 1), inc=(kc == nkc - 1))
            P.w("v", "te", te_t)
            rotb = b.pool[:, ROT + (qtp % 2) * 256: ROT + (qtp % 2) * 256 + 256]
            bw("v", war_rot[qtp % 2])
            ve_ot = P.cp("v", rotb, pvps, inc=True)
            war_pv = ("ve", ve_ot)
            P.w("t", "ve", ve_ot)

            bw("t", war_ptb)
            for tsub in range(2):
                te_t = P.tr(b.ptb[:, tsub * 128:(tsub + 1) * 128],
                            rotb[:, tsub * 128:(tsub + 1) * 128], b.ident[:],
                            inc=(tsub == 1))
            P.w("s", "te", te_t)
            for tsub in range(2):
                se_t = P.act(b.sqs[:, 0:128], b.ptb[:, tsub * 128:(tsub + 1) * 128],
                             AF.Square, accum=b.smf[:, SSQ + tsub:SSQ + tsub + 1],
                             inc=(tsub == 1))
            war_ptb = ("se", se_t)
            P.w("s", "se", se_t)
            se_t = P.act(b.smf[:, SSQ:SSQ + 2], b.smf[:, SSQ:SSQ + 2], AF.Sqrt,
                         bias=cpc(b, "eps"), scale=cpc(b, "inv_128"), inc=True)
            P.w("v", "se", se_t)
            vt = P.recip(b.smf[:, IVR:IVR + 2], b.smf[:, SSQ:SSQ + 2], inc=True)
            P.w("v", "ve", vt)
            ve_ivr = vt

            te_y_last = None
            P.w("t", "ve", ve_wo) if qtp == 0 else None
            for tsub in range(2):
                tt_tok = qtp * 2 + tsub
                for Dg in range(2):
                    sl = (tsub * 2 + Dg) % 2
                    ps = b.psc[:, 512:1024] if sl == 0 else b.pt32[:, 0:512]
                    bw("t", war_y[sl])
                    te_t = P.mm(ps, rotb[:, tsub * 128:(tsub + 1) * 128],
                                b.pool[:, RWO + Dg * 512: RWO + (Dg + 1) * 512], inc=True)
                    te_y_last = te_t
                    P.w("v", "te", te_t)
                    xsl = b.x[:, tt_tok * DM + Dg * 512: tt_tok * DM + (Dg + 1) * 512]
                    ve_t = P.stt("v", xsl, ps, b.smf[:, IVR + tsub:IVR + tsub + 1], xsl,
                                 ALU.mult, ALU.add, inc=True)
                    war_y[sl] = ("ve", ve_t)
            war_rot[qtp % 2] = ("te", te_y_last)
            ve_x = ve_t
        te_w_war = te_y_last
    return ve_x



_COMMON_NAMES = ["cpool", "ident_bf", "swap_bf", "ones_bf", "onesr_bf", "zeros_bf", "cmask", "rope_cos", "rope_sin"]


def _declare_dram(nc, st, parts):
    dr = {}
    def d(name, arr):
        dt = BF16 if arr.dtype == BF else F32
        dr[name] = nc.dram_tensor(name, list(arr.shape), dt, kind="ExternalInput").ap()
    for k in _COMMON_NAMES:
        d(k, st[k])
    if "l0ffn" in parts:
        d("w1T", st["l0ffn_w1T"]); d("w2T", st["l0ffn_w2T"])
        d("b1c", st["l0ffn_b1c"]); d("b2bc", st["l0ffn_b2bc"])
    if "l0ab" in parts:
        d("wthT", st["l0ab_wthT"]); d("wphT", st["l0ab_wphT"]); d("wgT", st["l0ab_wgT"])
    if "l1ffn" in parts:
        d("mwg", st["l1ffn_wg"]); d("mbg", st["l1ffn_bg_bc"]); d("mw1", st["l1ffn_w1"])
        d("mw2", st["l1ffn_w2"]); d("mb1c", st["l1ffn_b1c"]); d("mb2", st["l1ffn_b2"])
    if "l1aa" in parts:
        d("awqT", st["l1aa_wqT"]); d("awkT", st["l1aa_wkT"])
        d("awvT", st["l1aa_wvT"]); d("awoT", st["l1aa_woT"])
    if "l0aa" in parts:
        d("wqkv", st["l0aa_wqkv"]); d("wout", st["l0aa_wout"])
        d("gamma_bc", st["l0aa_gamma_bc"]); d("beta_bc", st["l0aa_beta_bc"])
        d("bout_bc", st["l0aa_bout_bc"])
    if "l1ab" in parts:
        d("bwqT", st["l1ab_wqT"]); d("bwkT", st["l1ab_wkT"])
        d("bwvT", st["l1ab_wvT"]); d("bwoT", st["l1ab_woT"])
        d("dlq1T", st["l1ab_lq1T"]); d("dlk1T", st["l1ab_lk1T"])
        d("dlq2T", st["l1ab_lq2T"]); d("dlk2T", st["l1ab_lk2T"])
        d("drms", st["l1ab_rms"])
    dr["x_in"] = nc.dram_tensor("x_in", [N, DM], F32, kind="ExternalInput").ap()
    dr["x_out"] = nc.dram_tensor("x_out", [N, DM], F32, kind="ExternalOutput").ap()
    if "lndump" in parts:
        dr["dbg_xnt"] = nc.dram_tensor("dbg_xnt", [128, ND * N], BF16, kind="ExternalOutput").ap()
        dr["dbg_st"] = nc.dram_tensor("dbg_st", [128, 192], F32, kind="ExternalOutput").ap()
    if "pooldump" in parts:
        dr["dbg_pool"] = nc.dram_tensor("dbg_pool", [128, POOL_COLS], BF16, kind="ExternalOutput").ap()
        dr["dbg_poolf"] = nc.dram_tensor("dbg_poolf", [128, POOLF_COLS], F32, kind="ExternalOutput").ap()
        dr["dbg_smf"] = nc.dram_tensor("dbg_smf", [128, 512], F32, kind="ExternalOutput").ap()
    return dr


def _in_map_common(st, parts):
    m = {k: st[k] for k in _COMMON_NAMES}
    if "l0ffn" in parts:
        m["w1T"] = st["l0ffn_w1T"]; m["w2T"] = st["l0ffn_w2T"]
        m["b1c"] = st["l0ffn_b1c"]; m["b2bc"] = st["l0ffn_b2bc"]
    if "l0ab" in parts:
        m["wthT"] = st["l0ab_wthT"]; m["wphT"] = st["l0ab_wphT"]; m["wgT"] = st["l0ab_wgT"]
    if "l1ffn" in parts:
        m["mwg"] = st["l1ffn_wg"]; m["mbg"] = st["l1ffn_bg_bc"]; m["mw1"] = st["l1ffn_w1"]
        m["mw2"] = st["l1ffn_w2"]; m["mb1c"] = st["l1ffn_b1c"]; m["mb2"] = st["l1ffn_b2"]
    if "l1aa" in parts:
        m["awqT"] = st["l1aa_wqT"]; m["awkT"] = st["l1aa_wkT"]
        m["awvT"] = st["l1aa_wvT"]; m["awoT"] = st["l1aa_woT"]
    if "l0aa" in parts:
        m["wqkv"] = st["l0aa_wqkv"]; m["wout"] = st["l0aa_wout"]
        m["gamma_bc"] = st["l0aa_gamma_bc"]; m["beta_bc"] = st["l0aa_beta_bc"]
        m["bout_bc"] = st["l0aa_bout_bc"]
    if "l1ab" in parts:
        m["bwqT"] = st["l1ab_wqT"]; m["bwkT"] = st["l1ab_wkT"]
        m["bwvT"] = st["l1ab_wvT"]; m["bwoT"] = st["l1ab_woT"]
        m["dlq1T"] = st["l1ab_lq1T"]; m["dlk1T"] = st["l1ab_lk1T"]
        m["dlq2T"] = st["l1ab_lq2T"]; m["dlk2T"] = st["l1ab_lk2T"]
        m["drms"] = st["l1ab_rms"]
    return m


def build_nc(st, parts):
    nc = bass.Bass("TRN2", target_bir_lowering=False)
    ctx = ExitStack()
    dr = _declare_dram(nc, st, parts)
    b = _alloc_bufs(nc, ctx)
    P = Prog(nc)

    for tt in range(NT):
        P.dma("y", b.x[:, tt * DM:(tt + 1) * DM], dr["x_in"][tt * 128:(tt + 1) * 128, :])
    P.dma("y", b.cp[:], dr["cpool"])
    P.dma("y", b.ident[:], dr["ident_bf"])
    P.dma("y", b.swap[:], dr["swap_bf"])
    P.dma("y", b.ones[:], dr["ones_bf"])
    P.dma("y", b.onesr[:], dr["onesr_bf"])
    P.dma("y", b.zeros[:], dr["zeros_bf"])
    P.dma("y", b.cmask[:], dr["cmask"])
    P.dma("y", b.cos[:], dr["rope_cos"])
    qs0 = P.dma("y", b.sin[:], dr["rope_sin"])
    for e in ("t", "v", "s"):
        P.w(e, "qs", qs0)

    ve_x = None
    for part in parts:
        if part == "pooldump":
            continue
        P.barrier()
        if part == "ln":
            ve_x = emit_ln_xnt(P, b, ve_x, gamma_beta=False)
        elif part == "lndump":
            xnt_ve = emit_ln_xnt(P, b, ve_x, gamma_beta=False)
            P.w("y", "ve", xnt_ve)
            tgt = P.dma("y", dr["dbg_xnt"], b.xnt[:])
            tgt = P.dma("y", dr["dbg_st"], b.st[:])
            P.w("y", "qs", tgt)
        elif part == "l0ffn":
            xnt_ve = emit_ln_xnt(P, b, ve_x, gamma_beta=False)
            ve_x = emit_l0ffn(P, b, dr, xnt_ve)
        elif part == "l0ab":
            xnt_ve = emit_ln_xnt(P, b, ve_x, gamma_beta=False)
            ve_x = emit_l0ab(P, b, dr, xnt_ve)
        elif part == "l1ffn":
            xnt_ve = emit_ln_xnt(P, b, ve_x, gamma_beta=False)
            ve_x = emit_l1ffn(P, b, dr, xnt_ve)
        elif part == "l1aa":
            xnt_ve = emit_ln_xnt(P, b, ve_x, gamma_beta=False)
            ve_x = emit_attn_a(
                P, b, dr, xnt_ve, causal=True, rope=False,
                wq_slice=lambda dr, dc, pp: dr["awqT"][dc * 128:(dc + 1) * 128, pp * 128:(pp + 1) * 128],
                wk_slice=lambda dr, dc, pp: dr["awkT"][dc * 128:(dc + 1) * 128, pp * 128:(pp + 1) * 128],
                wv_slice=lambda dr, dc, pp: dr["awvT"][dc * 128:(dc + 1) * 128, pp * 128:(pp + 1) * 128],
                wo_slice=lambda dr, pp: dr["awoT"][pp * 128:(pp + 1) * 128, :])
        elif part == "l1ab":
            xnt_ve = emit_ln_xnt(P, b, ve_x, gamma_beta=False)
            ve_x = emit_l1ab(P, b, dr, xnt_ve)
        elif part == "l0aa":
            t = P.dma("y", b.gamma[:], dr["gamma_bc"])
            t = P.dma("y", b.beta[:], dr["beta_bc"])
            t = P.dma("y", b.bout[:], dr["bout_bc"])
            P.w("v", "qs", t)
            xnt_ve = emit_ln_xnt(P, b, ve_x, gamma_beta=True)
            ve_x = emit_attn_a(
                P, b, dr, xnt_ve, causal=False, rope=True,
                wq_slice=lambda dr, dc, pp: dr["wqkv"][dc * 128:(dc + 1) * 128, pp * 384:pp * 384 + 128],
                wk_slice=lambda dr, dc, pp: dr["wqkv"][dc * 128:(dc + 1) * 128, pp * 384 + 128:pp * 384 + 256],
                wv_slice=lambda dr, dc, pp: dr["wqkv"][dc * 128:(dc + 1) * 128, pp * 384 + 256:pp * 384 + 384],
                wo_slice=lambda dr, pp: dr["wout"][pp * 128:(pp + 1) * 128, :], bout=True)
        else:
            raise NotImplementedError(part)
    if "pooldump" in parts:
        P.w("y", "ve", P.cnt["ve"])
        P.w("y", "se", P.cnt["se"])
        P.w("y", "te", P.cnt["te"])
        tgt = P.dma("y", dr["dbg_pool"], b.pool[:])
        tgt = P.dma("y", dr["dbg_smf"], b.smf[:])
        tgt = P.dma("y", dr["dbg_poolf"], b.poolf[:])
        P.w("y", "qs", tgt)

    P.w("y", "ve", ve_x)
    for tt in range(NT):
        tgt = P.dma("y", dr["x_out"][tt * 128:(tt + 1) * 128, :], b.x[:, tt * DM:(tt + 1) * DM])
    P.w("y", "qs", tgt)

    with nc.Block() as block:
        @block.tensor
        def _(eng):
            for fn in P.q["t"]:
                fn(eng)

        @block.vector
        def _(eng):
            for fn in P.q["v"]:
                fn(eng)

        @block.scalar
        def _(eng):
            for fn in P.q["s"]:
                fn(eng)

        @block.gpsimd
        def _(eng):
            for fn in P.q["g"]:
                fn(eng)

        @block.sync
        def _(eng):
            for fn in P.q["y"]:
                fn(eng)

    ctx.close()
    return nc, dr


def run_parts(inputs, parts, x_list):
    st = _stage_common(inputs)
    nc, dr = build_nc(st, parts)
    common = _in_map_common(st, parts)
    in_maps = []
    for c in range(len(x_list)):
        m = dict(common)
        m["x_in"] = np.ascontiguousarray(x_list[c], dtype=np.float32)
        in_maps.append(m)
    res = run_bass_kernel_spmd(nc, in_maps, core_ids=list(range(len(x_list))))
    if any(p in ("lndump", "pooldump") for p in parts):
        return res.results
    return [r["x_out"] for r in res.results]


def kernel(**inputs):
    x = np.asarray(inputs["L0_attn_a_x"], dtype=np.float32)
    outs = run_parts(inputs, ALL_PARTS, [x[c] for c in range(NCORES)])
    return np.stack(outs, axis=0)
